# version 08
# baseline (speedup 1.0000x reference)
import numpy as np
import ml_dtypes
from contextlib import ExitStack

import concourse.bass as bass
import concourse.mybir as mybir
from concourse.bass_utils import run_bass_kernel_spmd

F32 = mybir.dt.float32
BF16 = mybir.dt.bfloat16
I32 = mybir.dt.int32
OP = mybir.AluOpType
AF = mybir.ActivationFunctionType
AX = mybir.AxisListType.X

D = 1024
HID = 2048
NEXP = 16
NROUTED = 14
DS = 16
DIN = 2048
ZCH = DIN + DS
TOK = 2048
TT = 512
NT = TOK // TT
NDC = D // 128
NHC = HID // 128
LN_EPS = 1e-5
RMS_EPS = 1.1920929e-07
R_EPS = 1e-9

PH_MAMBA = True
PH_FFN2 = True
PH_PFX = True


class Sem:
    def __init__(self, nc, es, name):
        self.h = es.enter_context(nc.semaphore(name))
        self.n = 0

    def bump(self, k=1):
        self.n += k
        return self.n


class KB:

    def __init__(self, nc, es):
        self.nc = nc
        self.es = es
        self.prog = {"T": [], "V": [], "A": [], "G": [], "S": []}
        self.T = Sem(nc, es, "sT")
        self.V = Sem(nc, es, "sV")
        self.A = Sem(nc, es, "sA")
        self.G = Sem(nc, es, "sG")
        self.S = Sem(nc, es, "sS")
        self.GW2 = Sem(nc, es, "sGW2")
        self.MS = Sem(nc, es, "sMS")
        self.EX = Sem(nc, es, "sEX")
        self.PP = Sem(nc, es, "sPP")

    def op(self, eng, fn):
        self.prog[eng].append(fn)

    def wait(self, eng, sem, val):
        if val > 0:
            self.op(eng, lambda E, s=sem.h, v=val: E.wait_ge(s, v))


def sb(nc, es, name, shape, dtype):
    return es.enter_context(nc.sbuf_tensor(name, shape, dtype))


def psum(nc, es, name, shape):
    return es.enter_context(nc.psum_tensor(name, shape, F32))


def emit_ffn_pass1(k, B, npass):
    nc = k.nc
    row = lambda i: B["rows"][0:1, TT * i:TT * (i + 1)]
    rowb = lambda i: B["rowsb"][0:1, TT * i:TT * (i + 1)]
    t_tile_end = None
    for t in range(NT):
        if t_tile_end is not None:
            k.wait("V", k.T, t_tile_end)
        v_rb = []
        for d in range(NDC):
            k.op("V", lambda V, d=d, t=t: V.tensor_copy(
                out=B["rb"][:, TT * d:TT * (d + 1)],
                in_=B["r"][:, 2048 * d + TT * t: 2048 * d + TT * (t + 1)])
                .then_inc(k.V.h, 1))
            v_rb.append(k.V.bump())
        a_sq = []
        for d in range(NDC):
            k.wait("A", k.V, v_rb[d])
            k.op("A", lambda A, d=d: A.activation(
                out=B["sqb"][:, TT * d:TT * (d + 1)],
                in_=B["rb"][:, TT * d:TT * (d + 1)], func=AF.Square)
                .then_inc(k.A.h, 1))
            a_sq.append(k.A.bump())
        k.wait("T", k.V, v_rb[-1])
        for d in range(NDC):
            k.op("T", lambda T, d=d: T.matmul(
                out=B["pmisc"][0:1, :], lhsT=B["ones128b"][:, :],
                rhs=B["rb"][:, TT * d:TT * (d + 1)],
                start=(d == 0), stop=(d == NDC - 1))
                .then_inc(k.T.h, 1) if d == NDC - 1 else T.matmul(
                out=B["pmisc"][0:1, :], lhsT=B["ones128b"][:, :],
                rhs=B["rb"][:, TT * d:TT * (d + 1)],
                start=(d == 0), stop=(d == NDC - 1)))
        t_sumx = k.T.bump()
        k.wait("T", k.A, a_sq[-1])
        for d in range(NDC):
            last = d == NDC - 1
            k.op("T", lambda T, d=d, last=last: (
                T.matmul(out=B["pmisc2"][0:1, :], lhsT=B["ones128b"][:, :],
                         rhs=B["sqb"][:, TT * d:TT * (d + 1)],
                         start=(d == 0), stop=last).then_inc(k.T.h, 1)
                if last else
                T.matmul(out=B["pmisc2"][0:1, :], lhsT=B["ones128b"][:, :],
                         rhs=B["sqb"][:, TT * d:TT * (d + 1)],
                         start=(d == 0), stop=last)))
        t_sumsq = k.T.bump()
        k.wait("V", k.T, t_sumsq)
        def rowmath(V):
            V.tensor_scalar(out=row(0), in0=B["pmisc"][0:1, :],
                            scalar1=1.0 / D, scalar2=None, op0=OP.mult)
            V.tensor_scalar(out=row(1), in0=B["pmisc2"][0:1, :],
                            scalar1=1.0 / D, scalar2=None, op0=OP.mult)
            V.tensor_tensor(out=row(2), in0=row(0), in1=row(0), op=OP.mult)
            V.tensor_tensor(out=row(1), in0=row(1), in1=row(2),
                            op=OP.subtract).then_inc(k.V.h, 1)
        k.op("V", rowmath)
        v_var = k.V.bump()
        k.wait("A", k.V, v_var)
        k.op("A", lambda A: A.activation(
            out=row(2), in_=row(1), func=AF.Sqrt,
            bias=B["lneps"][0:1, 0:1]).then_inc(k.A.h, 1))
        a_sq1 = k.A.bump()
        k.wait("V", k.A, a_sq1)
        def rowmath2(V):
            V.reciprocal(out=row(3), in_=row(2))
            V.drain()
            V.tensor_tensor(out=row(2), in0=row(1), in1=row(3), op=OP.mult)
            V.tensor_tensor(out=row(2), in0=row(2), in1=row(3),
                            op=OP.mult).then_inc(k.V.h, 1)
        k.op("V", rowmath2)
        v_ms = k.V.bump()
        k.wait("A", k.V, v_ms)
        k.op("A", lambda A: A.activation(
            out=row(4), in_=row(2), func=AF.Sqrt,
            bias=B["rmseps"][0:1, 0:1]).then_inc(k.A.h, 1))
        a_sq2 = k.A.bump()
        k.wait("V", k.A, a_sq2)
        def rowmath3(V):
            V.reciprocal(out=row(4), in_=row(4))
            V.drain()
            V.tensor_tensor(out=row(4), in0=row(3), in1=row(4), op=OP.mult)
            V.tensor_copy(out=rowb(0), in_=row(4))
            V.tensor_tensor(out=row(2), in0=row(0), in1=row(4), op=OP.mult)
            V.tensor_scalar(out=row(2), in0=row(2),
                            scalar1=-1.0, scalar2=None, op0=OP.mult)
            V.tensor_copy(out=rowb(1), in_=row(2))
            V.tensor_copy(out=rowb(2), in_=row(0)).then_inc(k.V.h, 1)
        k.op("V", rowmath3)
        v_rows = k.V.bump()

        k.wait("T", k.V, v_rows)
        k.op("T", lambda T: T.matmul(out=B["pmisc"][:, :], lhsT=B["onesb"][0:1, :],
                                     rhs=rowb(0), start=True, stop=True)
             .then_inc(k.T.h, 1))
        t_cg = k.T.bump()
        k.op("T", lambda T: T.matmul(out=B["pmisc2"][:, :], lhsT=B["onesb"][0:1, :],
                                     rhs=rowb(1), start=True, stop=True)
             .then_inc(k.T.h, 1))
        t_bg = k.T.bump()
        k.wait("V", k.T, t_bg)
        def cgbg(V):
            V.tensor_copy(out=B["CgS"][:, :], in_=B["pmisc"][:, :])
            V.tensor_copy(out=B["BgS"][:, :], in_=B["pmisc2"][:, :]).then_inc(k.V.h, 1)
            V.drain()
        k.op("V", cgbg)
        v_cgbg = k.V.bump()

        for d in range(NDC):
            last = d == NDC - 1
            def gbuild(V, d=d, t=t, last=last):
                V.tensor_tensor(out=B["gt"][:, 0:TT],
                                in0=B["rb"][:, TT * d:TT * (d + 1)],
                                in1=B["CgS"][:, :], op=OP.mult)
                ins = V.tensor_tensor(out=B["g"][:, 2048 * d + TT * t:2048 * d + TT * (t + 1)],
                                      in0=B["gt"][:, 0:TT],
                                      in1=B["BgS"][:, :], op=OP.add)
                if last:
                    ins.then_inc(k.V.h, 1)
            k.op("V", gbuild)
        v_g = k.V.bump()

        k.wait("T", k.V, v_g)
        for d in range(NDC):
            k.op("T", lambda T, d=d: T.matmul(
                out=B["pmisc"][0:14, :], lhsT=B["routerb"][:, 14 * d:14 * (d + 1)],
                rhs=B["rb"][:, TT * d:TT * (d + 1)], start=(d == 0), stop=False))
        k.op("T", lambda T: T.matmul(
            out=B["pmisc"][0:14, :], lhsT=B["negcs"][0:1, :],
            rhs=rowb(2), start=False, stop=True).then_inc(k.T.h, 1))
        t_logit = k.T.bump()
        k.wait("V", k.T, t_logit)
        k.op("V", lambda V: V.tensor_copy(out=B["lgsb"][0:14, 0:TT],
                                          in_=B["pmisc"][0:14, :]).then_inc(k.V.h, 1))
        v_lg = k.V.bump()

        for j in range(4):
            k.wait("T", k.V, v_lg)
            k.op("T", lambda T, j=j: (
                T.transpose(out=B["pmisc2"][:, 15 * j:15 * j + 14],
                            in_=B["lgsb"][0:14, 128 * j:128 * (j + 1)],
                            identity=B["ident"][0:14, 0:14]),
                T.transpose(out=B["pmisc2"][:, 15 * j + 14:15 * j + 15],
                            in_=B["rows"][0:1, TT * 3 + 128 * j:TT * 3 + 128 * (j + 1)],
                            identity=B["ident"][0:1, 0:1]).then_inc(k.T.h, 1)))
            t_tr = k.T.bump()
            k.wait("V", k.T, t_tr)

            def tmmath(V, j=j):
                tm = B["tm"]
                V.tensor_copy(out=tm[:, 14:15], in_=B["pmisc2"][:, 15 * j + 14:15 * j + 15])
                V.drain()
                V.tensor_scalar(out=tm[:, 0:14], in0=B["pmisc2"][:, 15 * j:15 * j + 14],
                                scalar1=tm[:, 14:15],
                                scalar2=None, op0=OP.mult).then_inc(k.V.h, 1)
            k.op("V", tmmath)
            v_tm = k.V.bump()
            k.wait("A", k.V, v_tm)
            k.op("A", lambda A: A.activation(
                out=B["tm"][:, 16:30], in_=B["tm"][:, 0:14], func=AF.Exp,
                accum_out=B["tm"][:, 30:31]).then_inc(k.A.h, 1))
            a_exp = k.A.bump()
            k.wait("V", k.A, a_exp)

            def top2(V, j=j, t=t):
                tm = B["tm"]
                V.reciprocal(out=tm[:, 31:32], in_=tm[:, 30:31])
                V.drain()
                V.tensor_scalar(out=tm[:, 32:46], in0=tm[:, 16:30],
                                scalar1=tm[:, 31:32], scalar2=None, op0=OP.mult)
                V.drain()
                V.reduce_max(out=tm[:, 46:47], in_=tm[:, 32:46], axis=AX)
                V.drain()
                V.tensor_copy(out=B["vt8"][:, 0:1], in_=tm[:, 46:47])
                V.drain()
                V.match_replace(out=tm[:, 48:62], in_to_replace=B["vt8"][:, :],
                                in_values=tm[:, 32:46], imm_value=-1e30)
                V.drain()
                V.reduce_max(out=tm[:, 47:48], in_=tm[:, 48:62], axis=AX)
                V.drain()
                V.tensor_scalar(out=tm[:, 48:62], in0=tm[:, 32:46],
                                scalar1=tm[:, 47:48], scalar2=None, op0=OP.is_ge)
                V.drain()
                V.tensor_tensor(out=tm[:, 48:62], in0=tm[:, 32:46],
                                in1=tm[:, 48:62], op=OP.mult)
                V.drain()
                V.reduce_sum(out=tm[:, 62:63], in_=tm[:, 48:62], axis=AX)
                V.drain()
                V.tensor_scalar(out=tm[:, 62:63], in0=tm[:, 62:63],
                                scalar1=R_EPS, scalar2=2.0, op0=OP.add, op1=OP.mult)
                V.drain()
                V.reciprocal(out=tm[:, 63:64], in_=tm[:, 62:63])
                V.drain()
                V.memset(B["mtm"][:, 16 * j:16 * (j + 1)], 0.0)
                V.drain()
                V.tensor_scalar(out=B["mtm"][:, 16 * j:16 * j + 14], in0=tm[:, 48:62],
                                scalar1=tm[:, 63:64], scalar2=None,
                                op0=OP.mult).then_inc(k.V.h, 1)
            k.op("V", top2)
            v_m = k.V.bump()
            k.wait("T", k.V, v_m)
            k.op("T", lambda T, j=j: T.transpose(
                out=B["pmisc"][0:16, 128 * j:128 * (j + 1)],
                in_=B["mtm"][:, 16 * j:16 * (j + 1)],
                identity=B["ident"][:, :]).then_inc(k.T.h, 1))
            t_mb = k.T.bump()
            k.wait("V", k.T, t_mb)
            k.op("V", lambda V, j=j, t=t: V.tensor_copy(
                out=B["m_fm"][0:16, TT * t + 128 * j:TT * t + 128 * (j + 1)],
                in_=B["pmisc"][0:16, 128 * j:128 * (j + 1)]).then_inc(k.V.h, 1))
            k.V.bump()
        t_tile_end = k.T.n
    return k.V.n, k.T.n, k.A.n


def emit_ffn_pass2(k, B, W1dram, W2dram, wctr):
    nc = k.nc
    for e in range(NEXP):
        if wctr["t_w1_done"] is not None:
            k.wait("G", k.T, wctr["t_w1_done"])
        for half in range(2):
            k.op("G", lambda G, e=e, half=half: G.dma_start(
                out=B["w1s"][:, :].rearrange("p (c h) -> p c h", c=8)[:, 4 * half:4 * (half + 1), :],
                in_=W1dram[e].rearrange("(c p) h -> p c h", p=128)[:, 4 * half:4 * (half + 1), :])
                .then_inc(k.G.h, 16))
            k.G.bump(16)
        g_w1 = k.G.n
        if wctr["t_w2_done"] is not None:
            k.wait("G", k.T, wctr["t_w2_done"])
        for half in range(2):
            k.op("G", lambda G, e=e, half=half: G.dma_start(
                out=B["w2s"][:, :].rearrange("p (c h) -> p c h", c=16)[:, 8 * half:8 * (half + 1), :],
                in_=W2dram[e].rearrange("(c p) h -> p c h", p=128)[:, 8 * half:8 * (half + 1), :])
                .then_inc(k.GW2.h, 16))
            k.GW2.bump(16)
        g_w2 = k.GW2.n

        k.wait("V", k.G, g_w1)
        for c in range(NDC):
            last = c == NDC - 1
            def w1scale(V, e=e, c=c, last=last):
                ins = V.tensor_scalar(out=B["w1s"][:, 2048 * c:2048 * (c + 1)],
                                      in0=B["w1s"][:, 2048 * c:2048 * (c + 1)],
                                      scalar1=B["rms"][:, 8 * e + c:8 * e + c + 1],
                                      scalar2=None, op0=OP.mult)
                if last:
                    ins.then_inc(k.V.h, 1)
            k.op("V", w1scale)
        v_w1 = k.V.bump()

        for t in range(NT):
            shared = e >= NROUTED
            if not shared:
                sl = (e * NT + t) % 2
                prev = wctr["t_mrow"][sl]
                if prev is not None:
                    k.wait("S", k.T, prev)
                k.wait("S", k.V, wctr["v_g_ready"])
                k.op("S", lambda S, e=e, t=t, sl=sl: S.dma_start(
                    out=B["mrow"][0:1, TT * sl:TT * (sl + 1)],
                    in_=B["m_fm"][e:e + 1, TT * t:TT * (t + 1)]).then_inc(k.S.h, 16))
                s_mrow = k.S.bump(16)
                if wctr["v_ce_evac"] is not None:
                    k.wait("T", k.V, wctr["v_ce_evac"])
                k.wait("T", k.S, s_mrow)
                k.op("T", lambda T, sl=sl: T.matmul(
                    out=B["pmisc"][:, :], lhsT=B["onesb"][0:1, :],
                    rhs=B["mrow"][0:1, TT * sl:TT * (sl + 1)],
                    start=True, stop=True).then_inc(k.T.h, 1))
                t_ce = k.T.bump()
                wctr["t_mrow"][sl] = t_ce
                k.wait("V", k.T, t_ce)
                def ce_evac(V):
                    V.tensor_copy(out=B["Ce"][:, :],
                                  in_=B["pmisc"][:, :]).then_inc(k.V.h, 1)
                    V.drain()
                k.op("V", ce_evac)
                wctr["v_ce_evac"] = k.V.bump()

            k.wait("T", k.V, v_w1)
            if t == 0 and e == 0:
                k.wait("T", k.V, wctr["v_g_ready"])
            a_silu = []
            for m in range(NHC):
                pw = B["pw1a"] if m % 2 == 0 else B["pw1b"]
                if m >= 2:
                    k.wait("T", k.A, a_silu[m - 2])
                elif wctr.get("a_silu_tail"):
                    k.wait("T", k.A, wctr["a_silu_tail"][m])
                for c in range(NDC):
                    last = c == NDC - 1
                    def w1mm(T, e=e, t=t, m=m, c=c, last=last, pw=pw):
                        ins = T.matmul(
                            out=pw[:, :],
                            lhsT=B["w1s"][:, 2048 * c + 128 * m:2048 * c + 128 * (m + 1)],
                            rhs=B["g"][:, 2048 * c + TT * t:2048 * c + TT * (t + 1)],
                            start=(c == 0), stop=last)
                        if last:
                            ins.then_inc(k.T.h, 1)
                    k.op("T", w1mm)
                t_m1 = k.T.bump()
                k.wait("A", k.T, t_m1)
                k.op("A", lambda A, m=m, pw=pw: A.activation(
                    out=B["h"][:, TT * m:TT * (m + 1)], in_=pw[:, :],
                    func=AF.Silu).then_inc(k.A.h, 1))
                a_silu.append(k.A.bump())
            wctr["a_silu_tail"] = [a_silu[-2], a_silu[-1]]
            wctr["t_w1_done"] = k.T.n
            v_hm = []
            for m in range(NHC):
                k.wait("V", k.A, a_silu[m])
                if shared:
                    k.op("V", lambda V, m=m: V.tensor_scalar(
                        out=B["h"][:, TT * m:TT * (m + 1)],
                        in0=B["h"][:, TT * m:TT * (m + 1)],
                        scalar1=0.5, scalar2=None, op0=OP.mult).then_inc(k.V.h, 1))
                else:
                    k.op("V", lambda V, m=m: V.tensor_tensor(
                        out=B["h"][:, TT * m:TT * (m + 1)],
                        in0=B["h"][:, TT * m:TT * (m + 1)],
                        in1=B["Ce"][:, :], op=OP.mult).then_inc(k.V.h, 1))
                v_hm.append(k.V.bump())
            k.wait("T", k.GW2, g_w2)
            for j in range(4):
                pw = B["pw2a"] if j % 2 == 0 else B["pw2b"]
                if wctr["v_ev"][j % 2] is not None:
                    k.wait("T", k.V, wctr["v_ev"][j % 2])
                for m in range(NHC):
                    k.wait("T", k.V, v_hm[m])
                    for dd in range(2):
                        dmi = 2 * j + dd
                        last = (m == NHC - 1) and (dd == 1)
                        def w2mm(T, e=e, t=t, m=m, dmi=dmi, dd=dd, last=last, pw=pw):
                            ins = T.matmul(
                                out=pw[:, 512 * dd:512 * (dd + 1)],
                                lhsT=B["w2s"][:, 1024 * m + 128 * dmi:1024 * m + 128 * (dmi + 1)],
                                rhs=B["h"][:, TT * m:TT * (m + 1)],
                                start=(m == 0), stop=(m == NHC - 1))
                            if last:
                                ins.then_inc(k.T.h, 1)
                        k.op("T", w2mm)
                t_m2 = k.T.bump()
                k.wait("V", k.T, t_m2)
                def evac(V, j=j, t=t, pw=pw):
                    rsl = B["r"][:, :].rearrange("p (d x) -> p d x", d=NDC)[
                        :, 2 * j:2 * (j + 1), TT * t:TT * (t + 1)]
                    psl = pw[:, :].rearrange("p (d x) -> p d x", d=2)
                    V.tensor_tensor(out=rsl, in0=rsl, in1=psl, op=OP.add).then_inc(k.V.h, 1)
                k.op("V", evac)
                wctr["v_ev"][j % 2] = k.V.bump()
            wctr["t_w2_done"] = k.T.n
    return


def emit_output(k, B, o_dram):
    st_hist = []
    ev_hist = []
    for j in range(16):
        par = j % 2
        pm = B["pmisc"] if par == 0 else B["pmisc2"]
        v_e = None
        for grp in range(2):
            prev = 2 * (j - 2) + grp if grp == 0 else 2 * j + grp - 1
            if grp == 1:
                k.wait("T", k.V, ev_hist[2 * j])
            elif j >= 2:
                k.wait("T", k.V, ev_hist[2 * (j - 2) + 1])
            for dd in range(4):
                d = 4 * grp + dd
                def tr(T, j=j, d=d, dd=dd, pm=pm, last=(dd == 3)):
                    ins = T.transpose(out=pm[:, 128 * dd:128 * (dd + 1)],
                                      in_=B["r"][:, 2048 * d + 128 * j:2048 * d + 128 * (j + 1)],
                                      identity=B["ident"][:, :])
                    if last:
                        ins.then_inc(k.T.h, 1)
                k.op("T", tr)
            t_tr = k.T.bump()
            k.wait("V", k.T, t_tr)
            if grp == 0 and j >= 2:
                k.wait("V", k.S, st_hist[-1])
            def ev(V, grp=grp, pm=pm, par=par):
                V.tensor_copy(out=B["outT"][:, 1024 * par + 512 * grp:1024 * par + 512 * (grp + 1)],
                              in_=pm[:, :]).then_inc(k.V.h, 1)
            k.op("V", ev)
            ev_hist.append(k.V.bump())
            v_e = ev_hist[-1]
        k.wait("S", k.V, v_e)
        k.op("S", lambda S, j=j, par=par: S.dma_start(
            out=o_dram[128 * j:128 * (j + 1), :],
            in_=B["outT"][:, 1024 * par:1024 * (par + 1)]).then_inc(k.S.h, 16))
        st_hist.append(k.S.bump(16))
    k.wait("S", k.S, k.S.n)


def barrier(k):
    sems = [("T", k.T), ("V", k.V), ("A", k.A), ("G", k.G), ("S", k.S),
            ("EX", k.EX), ("GW2", k.GW2)]
    for ename in ["T", "V", "A", "G", "S"]:
        for sname, sem in sems:
            if sname != ename and sem.n > 0:
                k.wait(ename, sem, sem.n)


def emit_ln_pass(k, B):
    row = lambda i: B["rows"][0:1, TT * i:TT * (i + 1)]
    rowb = lambda i: B["rowsb"][0:1, TT * i:TT * (i + 1)]
    t_tile_end = None
    for t in range(NT):
        if t_tile_end is not None:
            k.wait("V", k.T, t_tile_end)
        v_rb = []
        for d in range(NDC):
            k.op("V", lambda V, d=d, t=t: V.tensor_copy(
                out=B["rb"][:, TT * d:TT * (d + 1)],
                in_=B["r"][:, 2048 * d + TT * t: 2048 * d + TT * (t + 1)])
                .then_inc(k.V.h, 1))
            v_rb.append(k.V.bump())
        a_sq = []
        for d in range(NDC):
            k.wait("A", k.V, v_rb[d])
            k.op("A", lambda A, d=d: A.activation(
                out=B["sqb"][:, TT * d:TT * (d + 1)],
                in_=B["rb"][:, TT * d:TT * (d + 1)], func=AF.Square)
                .then_inc(k.A.h, 1))
            a_sq.append(k.A.bump())
        k.wait("T", k.V, v_rb[-1])
        for d in range(NDC):
            last = d == NDC - 1
            k.op("T", lambda T, d=d, last=last: (
                T.matmul(out=B["pmisc"][0:1, :], lhsT=B["ones128b"][:, :],
                         rhs=B["rb"][:, TT * d:TT * (d + 1)],
                         start=(d == 0), stop=last).then_inc(k.T.h, 1)
                if last else
                T.matmul(out=B["pmisc"][0:1, :], lhsT=B["ones128b"][:, :],
                         rhs=B["rb"][:, TT * d:TT * (d + 1)],
                         start=(d == 0), stop=last)))
        k.T.bump()
        k.wait("T", k.A, a_sq[-1])
        for d in range(NDC):
            last = d == NDC - 1
            k.op("T", lambda T, d=d, last=last: (
                T.matmul(out=B["pmisc2"][0:1, :], lhsT=B["ones128b"][:, :],
                         rhs=B["sqb"][:, TT * d:TT * (d + 1)],
                         start=(d == 0), stop=last).then_inc(k.T.h, 1)
                if last else
                T.matmul(out=B["pmisc2"][0:1, :], lhsT=B["ones128b"][:, :],
                         rhs=B["sqb"][:, TT * d:TT * (d + 1)],
                         start=(d == 0), stop=last)))
        t_sumsq = k.T.bump()
        k.wait("V", k.T, t_sumsq)
        def rowmath(V):
            V.tensor_scalar(out=row(0), in0=B["pmisc"][0:1, :],
                            scalar1=1.0 / D, scalar2=None, op0=OP.mult)
            V.tensor_scalar(out=row(1), in0=B["pmisc2"][0:1, :],
                            scalar1=1.0 / D, scalar2=None, op0=OP.mult)
            V.tensor_tensor(out=row(2), in0=row(0), in1=row(0), op=OP.mult)
            V.tensor_tensor(out=row(1), in0=row(1), in1=row(2),
                            op=OP.subtract).then_inc(k.V.h, 1)
        k.op("V", rowmath)
        v_var = k.V.bump()
        k.wait("A", k.V, v_var)
        k.op("A", lambda A: A.activation(
            out=row(2), in_=row(1), func=AF.Sqrt,
            bias=B["lneps"][0:1, 0:1]).then_inc(k.A.h, 1))
        a_s1 = k.A.bump()
        k.wait("V", k.A, a_s1)
        def rowmath2(V):
            V.reciprocal(out=row(3), in_=row(2))
            V.drain()
            V.tensor_copy(out=rowb(0), in_=row(3))
            V.tensor_tensor(out=row(2), in0=row(0), in1=row(3), op=OP.mult)
            V.tensor_scalar(out=row(2), in0=row(2), scalar1=-1.0,
                            scalar2=None, op0=OP.mult)
            V.tensor_copy(out=rowb(1), in_=row(2)).then_inc(k.V.h, 1)
        k.op("V", rowmath2)
        v_rows = k.V.bump()
        k.wait("T", k.V, v_rows)
        k.op("T", lambda T: T.matmul(out=B["pmisc"][:, :], lhsT=B["onesb"][0:1, :],
                                     rhs=rowb(0), start=True, stop=True)
             .then_inc(k.T.h, 1))
        k.T.bump()
        k.op("T", lambda T: T.matmul(out=B["pmisc2"][:, :], lhsT=B["onesb"][0:1, :],
                                     rhs=rowb(1), start=True, stop=True)
             .then_inc(k.T.h, 1))
        t_bg = k.T.bump()
        k.wait("V", k.T, t_bg)
        def cgbg(V):
            V.tensor_copy(out=B["CgS"][:, :], in_=B["pmisc"][:, :])
            V.tensor_copy(out=B["BgS"][:, :], in_=B["pmisc2"][:, :]).then_inc(k.V.h, 1)
            V.drain()
        k.op("V", cgbg)
        v_cb = k.V.bump()
        for d in range(NDC):
            last = d == NDC - 1
            def gbuild(V, d=d, t=t, last=last):
                V.tensor_tensor(out=B["gt"][:, 0:TT],
                                in0=B["rb"][:, TT * d:TT * (d + 1)],
                                in1=B["CgS"][:, :], op=OP.mult)
                ins = V.tensor_tensor(
                    out=B["g"][:, 2048 * d + TT * t:2048 * d + TT * (t + 1)],
                    in0=B["gt"][:, 0:TT], in1=B["BgS"][:, :], op=OP.add)
                if last:
                    ins.then_inc(k.V.h, 1)
            k.op("V", gbuild)
        k.V.bump()
        t_tile_end = k.T.n
    def myext(V):
        for d in range(NDC):
            ins = V.tensor_copy(out=B["myxn3"][:, 3 * d:3 * (d + 1)],
                                in_=B["g"][:, 2048 * d + 2045:2048 * d + 2048])
        ins.then_inc(k.V.h, 1)
    k.op("V", myext)
    return k.V.bump()


def g_poll(G, R, k, flag_dram, expect):
    G.reg_mov(R["rdone"], 1)
    with G.While(R["rdone"]):
        G.reg_add(R["rcnt"], R["rcnt"], 16)
        G.dma_start(out=R["fv"][:, :], in_=flag_dram[:, :]).then_inc(k.PP.h, 16)
        G.wait_ge(k.PP.h, R["rcnt"])
        G.load(R["rflag"], R["fv"][0:1, 0:1])
        G.reg_alu(R["rdone"], R["rflag"], expect, OP.is_lt)


def emit_exchange(k, B, R, sender_lane, pairs, flag_dram, flag_val):
    n = len(pairs)
    base = k.EX.n

    msv = k.MS.bump()

    def go(G):
        with G.If_eq(R["rlane"], sender_lane):
            cnt = base
            for (slot, src, dst) in pairs:
                G.dma_start(out=slot, in_=src).then_inc(k.EX.h, 16)
                cnt += 16
                G.wait_ge(k.EX.h, cnt)
            G.memset(R["fl"][:, :], flag_val).then_inc(k.MS.h, 1)
            G.wait_ge(k.MS.h, msv)
            G.dma_start(out=flag_dram[:, :], in_=R["fl"][:, :]).then_inc(k.EX.h, 16)
            G.wait_ge(k.EX.h, base + 16 * (n + 1))
        with G.Else():
            G.memset(R["fv"][:, :], 0).then_inc(k.MS.h, 1)
            G.wait_ge(k.MS.h, msv)
            g_poll(G, R, k, flag_dram, flag_val)
            cnt = base
            for (slot, src, dst) in pairs:
                G.dma_start(out=dst, in_=slot).then_inc(k.EX.h, 16)
                cnt += 16
                G.wait_ge(k.EX.h, cnt)
            G.dma_start(out=R["fv"][:, :], in_=flag_dram[:, :]).then_inc(k.EX.h, 16)
            G.wait_ge(k.EX.h, base + 16 * (n + 1))
    k.op("G", go)
    return k.EX.bump(16 * (n + 1))


def emit_front(k, B, R, SHM, is_pfx):
    w2f = B["w2s"][:, :].bitcast(F32)
    Asb = w2f[:, 0:2048]
    Bxsb = w2f[:, 2048:4096]
    hsb = w2f[:, 4096:6144]
    z16sb = w2f[:, 6144:8192]
    xcpre = B["h"][:, 0:2051]
    xcraw = B["h"][:, 2560:4608]
    xcs = B["h"][:, 4608:6656]
    cb = B["carrybuf"]
    hmkey = "pfxhm" if is_pfx else "ownhm"

    for q in range(NT):
        for c in range(NDC):
            last = c == NDC - 1
            def zmm(T, q=q, c=c, last=last):
                ins = T.matmul(out=B["pmisc"][0:16, :],
                               lhsT=B["wzT"][:, 16 * c:16 * (c + 1)],
                               rhs=B["g"][:, 2048 * c + TT * q:2048 * c + TT * (q + 1)],
                               start=(c == 0), stop=last)
                if last:
                    ins.then_inc(k.T.h, 1)
            k.op("T", zmm)
        t_z = k.T.bump()
        k.wait("A", k.T, t_z)
        k.op("A", lambda A, q=q: A.activation(
            out=z16sb[0:16, TT * q:TT * (q + 1)], in_=B["pmisc"][0:16, :],
            func=AF.Identity, bias=B["binb"][0:16, 16:17]).then_inc(k.A.h, 1))
        a_z = k.A.bump()
        k.wait("T", k.A, a_z)

    a_evac = [None, None]
    v_conv_last = None
    for m in range(NHC):
        for q in range(NT):
            pw = B["pw1a"] if (m * NT + q) % 2 == 0 else B["pw1b"]
            par = (m * NT + q) % 2
            if a_evac[par] is not None:
                k.wait("T", k.A, a_evac[par])
            for c in range(NDC):
                last = c == NDC - 1
                def wmm(T, m=m, q=q, c=c, last=last, pw=pw):
                    ins = T.matmul(
                        out=pw[:, :],
                        lhsT=B["w1s"][:, 2048 * c + 128 * m:2048 * c + 128 * (m + 1)],
                        rhs=B["g"][:, 2048 * c + TT * q:2048 * c + TT * (q + 1)],
                        start=(c == 0), stop=last)
                    if last:
                        ins.then_inc(k.T.h, 1)
                k.op("T", wmm)
            t_mq = k.T.bump()
            k.wait("A", k.T, t_mq)
            k.op("A", lambda A, m=m, q=q, pw=pw: A.activation(
                out=xcpre[:, 3 + TT * q:3 + TT * (q + 1)], in_=pw[:, :],
                func=AF.Identity, bias=B["binb"][:, m:m + 1]).then_inc(k.A.h, 1))
            a_evac[par] = k.A.bump()
        for c in range(NDC):
            last = c == NDC - 1
            def hmm(T, m=m, c=c, last=last):
                ins = T.matmul(out=B["pmisc2"][:, 0:3],
                               lhsT=B["w1s"][:, 2048 * c + 128 * m:2048 * c + 128 * (m + 1)],
                               rhs=B["haloxn"][:, 3 * c:3 * (c + 1)],
                               start=(c == 0), stop=last)
                if last:
                    ins.then_inc(k.T.h, 1)
            k.op("T", hmm)
        t_h = k.T.bump()
        k.wait("A", k.T, t_h)
        k.op("A", lambda A, m=m: A.activation(
            out=xcpre[:, 0:3], in_=B["pmisc2"][:, 0:3],
            func=AF.Identity, bias=B["binb"][:, m:m + 1]).then_inc(k.A.h, 1))
        a_h = k.A.bump()
        k.wait("V", k.A, max(a_h, a_evac[0] or 0, a_evac[1] or 0))
        def conv(V, m=m):
            V.tensor_scalar(out=xcpre[:, 0:3], in0=xcpre[:, 0:3],
                            scalar1=B[hmkey][:, 0:1], scalar2=None, op0=OP.mult)
            V.drain()
            V.tensor_scalar(out=xcraw[:, :], in0=xcpre[:, 0:2048],
                            scalar1=B["wconv"][:, 4 * m:4 * m + 1],
                            scalar2=None, op0=OP.mult)
            V.drain()
            for dd in range(1, 4):
                V.scalar_tensor_tensor(out=xcraw[:, :], in0=xcpre[:, dd:2048 + dd],
                                       scalar=B["wconv"][:, 4 * m + dd:4 * m + dd + 1],
                                       in1=xcraw[:, :], op0=OP.mult, op1=OP.add)
                V.drain()
            V.tensor_copy(out=xcraw[:, 0:1], in_=xcraw[:, 0:1]).then_inc(k.V.h, 1)
        k.op("V", conv)
        v_conv = k.V.bump()
        k.wait("A", k.V, v_conv)
        k.op("A", lambda A, m=m: A.activation(
            out=xcs[:, :], in_=xcraw[:, :], func=AF.Silu,
            bias=B["bconv"][:, m:m + 1]).then_inc(k.A.h, 1))
        a_silu = k.A.bump()
        if m == 0:
            k.op("A", lambda A: A.activation(
                out=B["m_fm"][0:16, :], in_=xcraw[0:16, :], func=AF.Silu,
                bias=B["bconv"][0:16, 0:1]).then_inc(k.A.h, 1))
            k.A.bump()
        k.wait("T", k.A, a_silu)
        for q in range(NT):
            pw = B["pw2a"] if q < 2 else B["pw2b"]
            qq = q % 2
            last = m == NHC - 1
            qlast = q == NT - 1
            def xpmm(T, m=m, q=q, qq=qq, pw=pw, last=last, qlast=qlast):
                ins = T.matmul(out=pw[0:80, TT * qq:TT * (qq + 1)],
                               lhsT=B["xpd"][:, 80 * m:80 * (m + 1)],
                               rhs=xcs[:, TT * q:TT * (q + 1)],
                               start=(m == 0), stop=last)
                if qlast:
                    ins.then_inc(k.T.h, 1)
            k.op("T", xpmm)
            if qlast:
                t_xp_last = k.T.bump()
        v_conv_last = v_conv
    t_xpd = k.T.n

    for q in range(NT):
        pw = B["pw2a"] if q < 2 else B["pw2b"]
        qq = q % 2
        P = pw[:, TT * qq:TT * (qq + 1)]
        sA = B["s16"][0:16, 0:TT]
        sB = B["s16"][0:16, TT:2 * TT]
        sC = B["s16"][0:16, 2 * TT:3 * TT]
        k.wait("A", k.T, t_xpd)
        if q > 0:
            k.wait("A", k.V, k.V.n)
        def ab_a1(A, P=P, sA=sA, sB=sB, sC=sC):
            A.activation(out=sA, in_=P[64:80, :], func=AF.Identity, bias=B["bdt"][64:80, 0:1])
            A.activation(out=sB, in_=P[0:16, :], func=AF.Abs)
            A.activation(out=sC, in_=sA, func=AF.Abs)
            A.activation(out=sC, in_=sC, func=AF.Exp, scale=-1.0)
            A.activation(out=sC, in_=sC, func=AF.Ln,
                         bias=B["one16"][0:16, 0:1]).then_inc(k.A.h, 1)
        k.op("A", ab_a1)
        a_1 = k.A.bump()
        k.wait("V", k.A, a_1)
        def ab_v1(V, q=q, P=P, sA=sA, sB=sB, sC=sC):
            V.tensor_scalar(out=sA, in0=sA, scalar1=0.0, scalar2=None, op0=OP.max)
            V.drain()
            V.tensor_tensor(out=sA, in0=sA, in1=sC, op=OP.add)
            V.drain()
            V.tensor_tensor(out=sB, in0=sB, in1=sA, op=OP.mult)
            V.drain()
            V.tensor_copy(out=sC, in_=P[32:48, :])
            V.drain()
            V.tensor_tensor(out=sC, in0=sC, in1=sA, op=OP.mult)
            V.drain()
            V.tensor_tensor(out=Bxsb[0:16, TT * q:TT * (q + 1)], in0=sC,
                            in1=B["m_fm"][0:16, TT * q:TT * (q + 1)],
                            op=OP.mult).then_inc(k.V.h, 1)
        k.op("V", ab_v1)
        v_1 = k.V.bump()
        k.wait("A", k.V, v_1)
        k.op("A", lambda A, q=q, sB=sB: A.activation(
            out=Asb[0:16, TT * q:TT * (q + 1)], in_=sB, func=AF.Exp,
            scale=-1.0).then_inc(k.A.h, 1))
        k.A.bump()

    k.wait("V", k.A, k.A.n)
    if is_pfx:
        k.op("V", lambda V: V.memset(cb[0:16, 8:9], 0.0).then_inc(k.V.h, 1))
        k.V.bump()
    def scan1(V):
        V.drain()
        V.tensor_tensor_scan(out=hsb[0:16, :], data0=Asb[0:16, :],
                             data1=Bxsb[0:16, :], initial=cb[0:16, 8:9],
                             op0=OP.mult, op1=OP.add)
        V.drain()
        V.tensor_copy(out=cb[0:16, 0:1], in_=hsb[0:16, 2047:2048]).then_inc(k.V.h, 1)
        V.drain()
    k.op("V", scan1)
    v_s1 = k.V.bump()
    k.wait("G", k.V, v_s1)
    if is_pfx:
        ex = emit_exchange(k, B, R, 0, [(SHM["c16a"][0:16, 0:8], cb[0:16, 0:8], cb[0:16, 8:16])],
                           SHM["f01"], 2)
    else:
        ex = emit_exchange(k, B, R, 0, [(SHM["c16c"][0:16, 0:8], cb[0:16, 0:8], cb[0:16, 8:16])],
                           SHM["f01"], 4)
    k.wait("V", k.EX, ex)
    def scan2(V):
        V.tensor_tensor_scan(out=hsb[0:16, :], data0=Asb[0:16, :],
                             data1=Bxsb[0:16, :], initial=cb[0:16, 8:9],
                             op0=OP.mult, op1=OP.add)
        V.drain()
        V.tensor_copy(out=cb[0:16, 1:2], in_=hsb[0:16, 2047:2048]).then_inc(k.V.h, 1)
        V.drain()
    k.op("V", scan2)
    v_s2 = k.V.bump()

    if is_pfx:
        def endm(V):
            V.tensor_tensor(out=cb[0:16, 0:1], in0=cb[0:16, 1:2],
                            in1=B["hm16"][0:16, 0:1], op=OP.mult).then_inc(k.V.h, 1)
            V.drain()
        k.op("V", endm)
        v_em = k.V.bump()
        k.wait("G", k.V, v_em)
        ex3 = emit_exchange(k, B, R, 1,
                            [(SHM["c16b"][0:16, 0:8], cb[0:16, 0:8], cb[0:16, 8:16]),
                             (SHM["sx3"][:, :], B["myxn3"][:, :], B["haloxn"][:, :])],
                            SHM["f10"], 1)
        return ex3
    else:
        k.wait("V", k.T, k.T.n)
        def y16(V):
            V.tensor_tensor(out=B["m_fm"][0:16, :], in0=hsb[0:16, :],
                            in1=z16sb[0:16, :], op=OP.mult).then_inc(k.V.h, 1)
            V.drain()
        k.op("V", y16)
        v_y = k.V.bump()
        k.wait("T", k.V, v_y)
        v_ev = [None, None]
        for q in range(NT):
            for dd in range(NDC):
                par = (q * NDC + dd) % 2
                pw = B["pw1a"] if par == 0 else B["pw1b"]
                if v_ev[par] is not None:
                    k.wait("T", k.V, v_ev[par])
                def womm(T, q=q, dd=dd, pw=pw):
                    T.matmul(out=pw[:, :], lhsT=B["wout"][0:16, 128 * dd:128 * (dd + 1)],
                             rhs=B["m_fm"][0:16, TT * q:TT * (q + 1)],
                             start=True, stop=True).then_inc(k.T.h, 1)
                k.op("T", womm)
                t_wo = k.T.bump()
                k.wait("V", k.T, t_wo)
                def woev(V, q=q, dd=dd, pw=pw):
                    rsl = B["r"][:, 2048 * dd + TT * q:2048 * dd + TT * (q + 1)]
                    V.tensor_tensor(out=rsl, in0=rsl, in1=pw[:, :],
                                    op=OP.add).then_inc(k.V.h, 1)
                k.op("V", woev)
                v_ev[par] = k.V.bump()
        return k.V.n


def build_program():
    nc = bass.Bass("TRN2", target_bir_lowering=False, detect_race_conditions=False)
    xo = nc.dram_tensor("xo", [D, TOK], F32, kind="ExternalInput")
    xp = nc.dram_tensor("xp", [D, TOK], F32, kind="ExternalInput")
    w1d = nc.dram_tensor("w1d", [NEXP, D, HID], F32, kind="ExternalInput")
    w2d = nc.dram_tensor("w2d", [NEXP, HID, D], F32, kind="ExternalInput")
    rmsd = nc.dram_tensor("rmsd", [128, 128], F32, kind="ExternalInput")
    routerd = nc.dram_tensor("routerd", [D, 14], F32, kind="ExternalInput")
    identd = nc.dram_tensor("identd", [128, 128], F32, kind="ExternalInput")
    wind = nc.dram_tensor("wind", [D, DIN], F32, kind="ExternalInput")
    wzd = nc.dram_tensor("wzd", [D, 16], F32, kind="ExternalInput")
    xpdd = nc.dram_tensor("xpdd", [HID, 80], F32, kind="ExternalInput")
    wconvd = nc.dram_tensor("wconvd", [128, 64], F32, kind="ExternalInput")
    bconvd = nc.dram_tensor("bconvd", [128, 16], F32, kind="ExternalInput")
    binbd = nc.dram_tensor("binbd", [128, 17], F32, kind="ExternalInput")
    bdtd = nc.dram_tensor("bdtd", [128, 1], F32, kind="ExternalInput")
    woutd = nc.dram_tensor("woutd", [16, 1024], F32, kind="ExternalInput")
    laned = nc.dram_tensor("laned", [1, 8], I32, kind="ExternalInput")
    ownhmd = nc.dram_tensor("ownhmd", [128, 1], F32, kind="ExternalInput")
    pfxhmd = nc.dram_tensor("pfxhmd", [128, 1], F32, kind="ExternalInput")
    hm16d = nc.dram_tensor("hm16d", [16, 1], F32, kind="ExternalInput")
    o = nc.dram_tensor("o", [TOK, D], F32, kind="ExternalOutput")
    SHM = {
        "sx1": nc.dram_tensor("sx1", [128, 24], BF16, addr_space="Shared").ap(),
        "sx3": nc.dram_tensor("sx3", [128, 24], BF16, addr_space="Shared").ap(),
        "sx4": nc.dram_tensor("sx4", [128, 24], BF16, addr_space="Shared").ap(),
        "c16a": nc.dram_tensor("c16a", [16, 8], F32, addr_space="Shared").ap(),
        "c16b": nc.dram_tensor("c16b", [16, 8], F32, addr_space="Shared").ap(),
        "c16c": nc.dram_tensor("c16c", [16, 8], F32, addr_space="Shared").ap(),
        "f01": nc.dram_tensor("f01", [1, 8], I32, addr_space="Shared").ap(),
        "f10": nc.dram_tensor("f10", [1, 8], I32, addr_space="Shared").ap(),
    }

    es = ExitStack()
    with es:
        k = KB(nc, es)
        B = {}
        B["r"] = sb(nc, es, "r", [128, 8 * 2048], F32)
        B["g"] = sb(nc, es, "g", [128, 8 * 2048], BF16)
        B["w1s"] = sb(nc, es, "w1s", [128, 16384], BF16)
        B["w2s"] = sb(nc, es, "w2s", [128, 16384], BF16)
        B["h"] = sb(nc, es, "h", [128, 16 * TT], BF16)
        B["rb"] = B["h"][:, 0:8 * TT]
        B["sqb"] = B["h"][:, 8 * TT:16 * TT]
        B["outT"] = B["h"][:, :].bitcast(F32)[:, 0:2048]
        B["m_fm"] = sb(nc, es, "m_fm", [16, 2048], BF16)
        B["CgS"] = sb(nc, es, "CgS", [128, TT], BF16)
        B["BgS"] = sb(nc, es, "BgS", [128, TT], BF16)
        B["Ce"] = sb(nc, es, "Ce", [128, TT], BF16)
        B["gt"] = B["Ce"]
        B["scr"] = sb(nc, es, "scr", [16, 5 * TT], F32)
        B["rows"] = B["scr"]
        B["s16"] = B["scr"]
        B["rowsb"] = sb(nc, es, "rowsb", [1, 3 * TT], BF16)
        B["mrow"] = sb(nc, es, "mrow", [1, 2 * TT], BF16)
        B["lgsb"] = sb(nc, es, "lgsb", [14, TT], F32)
        B["tm"] = sb(nc, es, "tm", [128, 64], F32)
        B["mtm"] = sb(nc, es, "mtm", [128, 64], F32)
        B["vt8"] = sb(nc, es, "vt8", [128, 8], F32)
        B["ident"] = sb(nc, es, "ident", [128, 128], F32)
        B["ones128b"] = sb(nc, es, "ones128b", [128, 1], BF16)
        B["onesb"] = sb(nc, es, "onesb", [1, 128], BF16)
        B["routerb"] = sb(nc, es, "routerb", [128, 14 * NDC], BF16)
        B["negcs"] = sb(nc, es, "negcs", [1, 14], BF16)
        B["rms"] = sb(nc, es, "rms", [128, 128], F32)
        B["lneps"] = sb(nc, es, "lneps", [1, 1], F32)
        B["rmseps"] = sb(nc, es, "rmseps", [1, 1], F32)
        B["wzT"] = sb(nc, es, "wzT", [128, 128], BF16)
        B["xpd"] = sb(nc, es, "xpd", [128, 16 * 80], BF16)
        B["wconv"] = sb(nc, es, "wconv", [128, 64], F32)
        B["bconv"] = sb(nc, es, "bconv", [128, 16], F32)
        B["binb"] = sb(nc, es, "binb", [128, 17], F32)
        B["bdt"] = sb(nc, es, "bdt", [128, 1], F32)
        B["wout"] = sb(nc, es, "wout", [16, 1024], BF16)
        B["myxn3"] = sb(nc, es, "myxn3", [128, 24], BF16)
        B["haloxn"] = sb(nc, es, "haloxn", [128, 24], BF16)
        B["carrybuf"] = sb(nc, es, "carrybuf", [16, 16], F32)
        B["one16"] = sb(nc, es, "one16", [16, 1], F32)
        B["hm16"] = sb(nc, es, "hm16", [16, 1], F32)
        B["ownhm"] = sb(nc, es, "ownhm", [128, 1], F32)
        B["pfxhm"] = sb(nc, es, "pfxhm", [128, 1], F32)
        B["lansb"] = sb(nc, es, "lansb", [1, 8], I32)
        R = {"fl": sb(nc, es, "flg", [1, 8], I32),
             "fv": sb(nc, es, "fvw", [1, 8], I32)}
        B["pw1a"] = psum(nc, es, "pw1a", [128, 512])
        B["pw1b"] = psum(nc, es, "pw1b", [128, 512])
        B["pw2a"] = psum(nc, es, "pw2a", [128, 1024])
        B["pw2b"] = psum(nc, es, "pw2b", [128, 1024])
        B["pmisc"] = psum(nc, es, "pmisc", [128, 512])
        B["pmisc2"] = psum(nc, es, "pmisc2", [128, 512])

        for dst, srcd in [(B["ident"], identd), (B["rms"], rmsd),
                          (B["wconv"], wconvd), (B["bconv"], bconvd),
                          (B["binb"], binbd), (B["bdt"], bdtd),
                          (B["lansb"], laned), (B["ownhm"], ownhmd),
                          (B["pfxhm"], pfxhmd), (B["hm16"], hm16d)]:
            k.op("S", lambda S, dst=dst, srcd=srcd: S.dma_start(
                out=dst[:, :], in_=srcd[:, :]).then_inc(k.S.h, 16))
            k.S.bump(16)
        k.op("G", lambda G: G.dma_start(
            out=B["routerb"][:, :].rearrange("p (c e) -> p c e", c=8),
            in_=routerd[:, :].rearrange("(c p) e -> p c e", p=128)).then_inc(k.G.h, 16))
        k.G.bump(16)
        k.op("G", lambda G: G.dma_start(
            out=B["wzT"][:, :].rearrange("p (c e) -> p c e", c=8),
            in_=wzd[:, :].rearrange("(c p) e -> p c e", p=128)).then_inc(k.G.h, 16))
        k.G.bump(16)
        k.op("G", lambda G: G.dma_start(
            out=B["xpd"][:, :].rearrange("p (c e) -> p c e", c=16),
            in_=xpdd[:, :].rearrange("(c p) e -> p c e", p=128)).then_inc(k.G.h, 16))
        k.G.bump(16)
        k.op("G", lambda G: G.dma_start(
            out=B["wout"][:, :], in_=woutd[:, :]).then_inc(k.G.h, 16))
        k.G.bump(16)
        k.op("S", lambda S: S.dma_start(
            out=B["r"][:, :].rearrange("p (c t) -> p c t", c=8),
            in_=xp[:, :].rearrange("(c p) t -> p c t", p=128))
            .then_inc(k.S.h, 16))
        s_r = k.S.bump(16)
        def init_v(V):
            V.memset(B["ones128b"][:, :], 1.0)
            V.memset(B["onesb"][:, :], 1.0)
            V.memset(B["vt8"][:, :], -5.0)
            V.memset(B["one16"][:, :], 1.0)
            V.memset(B["haloxn"][:, :], 0.0)
            V.memset(B["carrybuf"][:, :], 0.0)
            V.memset(B["lneps"][:, :], LN_EPS)
            V.memset(B["rmseps"][:, :], RMS_EPS).then_inc(k.V.h, 1)
        k.op("V", init_v)
        v_init = k.V.bump()
        _sn_ginit = k.S.n
        _msv = None
        def ginit(G, _sn=_sn_ginit):
            R["rlane"] = G.alloc_register("rlane")
            R["rcnt"] = G.alloc_register("rcnt")
            R["rflag"] = G.alloc_register("rflag")
            R["rdone"] = G.alloc_register("rdone")
            G.wait_ge(k.S.h, _sn)
            G.load(R["rlane"], B["lansb"][0:1, 0:1])
            G.reg_mov(R["rcnt"], 0)
            G.memset(R["fl"][:, :], 0).then_inc(k.MS.h, 1)
            G.wait_ge(k.MS.h, _msv_v[0])
            G.dma_start(out=SHM["f01"][:, :], in_=R["fl"][:, :]).then_inc(k.G.h, 16)
            G.dma_start(out=SHM["f10"][:, :], in_=R["fl"][:, :]).then_inc(k.G.h, 16)
            G.wait_ge(k.G.h, 64 + 32)
        _msv_v = [k.MS.bump()]
        k.op("G", ginit)
        k.G.bump(32)
        k.wait("T", k.G, k.G.n)
        k.wait("T", k.V, v_init)
        def cs_mm(T):
            for c in range(NDC):
                ins = T.matmul(out=B["pmisc"][0:1, 0:14], lhsT=B["ones128b"][:, :],
                               rhs=B["routerb"][:, 14 * c:14 * (c + 1)],
                               start=(c == 0), stop=(c == NDC - 1))
            ins.then_inc(k.T.h, 1)
        k.op("T", cs_mm)
        t_cs = k.T.bump()
        k.wait("V", k.T, t_cs)
        k.op("V", lambda V: V.tensor_scalar(out=B["negcs"][0:1, :], in0=B["pmisc"][0:1, 0:14],
                                            scalar1=-1.0, scalar2=None, op0=OP.mult)
             .then_inc(k.V.h, 1))
        v_cs = k.V.bump()
        k.wait("V", k.S, s_r)
        barrier(k)

        def ffn_call(first):
            emit_ffn_pass1(k, B, 1)
            wctr = {"t_w1_done": None, "t_w2_done": None, "v_ce_evac": k.V.n,
                    "v_ev": [None, None], "v_g_ready": k.V.n,
                    "t_mrow": [None, None]}
            emit_ffn_pass2(k, B, [w1d[e] for e in range(NEXP)],
                           [w2d[e] for e in range(NEXP)], wctr)

        def load_winT():
            for half in range(2):
                k.op("G", lambda G, half=half: G.dma_start(
                    out=B["w1s"][:, :].rearrange("p (c h) -> p c h", c=8)[:, 4 * half:4 * (half + 1), :],
                    in_=wind[:, :].rearrange("(c p) h -> p c h", p=128)[:, 4 * half:4 * (half + 1), :])
                    .then_inc(k.G.h, 16))
                k.G.bump(16)
            k.wait("T", k.G, k.G.n)
            k.wait("A", k.G, k.G.n)

        if PH_PFX:
            ffn_call(True)
            barrier(k)
            emit_ln_pass(k, B)
            barrier(k)
            ex1 = emit_exchange(k, B, R, 0, [(SHM["sx1"][:, :], B["myxn3"][:, :],
                                              B["haloxn"][:, :])], SHM["f01"], 1)
            k.wait("V", k.EX, ex1)
            k.wait("T", k.EX, ex1)
            load_winT()
            emit_front(k, B, R, SHM, True)
            barrier(k)
        k.op("S", lambda S: S.dma_start(
            out=B["r"][:, :].rearrange("p (c t) -> p c t", c=8),
            in_=xo[:, :].rearrange("(c p) t -> p c t", p=128))
            .then_inc(k.S.h, 16))
        k.S.bump(16)
        barrier(k)
        ffn_call(False)
        if PH_MAMBA:
            barrier(k)
            emit_ln_pass(k, B)
            barrier(k)
            ex4 = emit_exchange(k, B, R, 0, [(SHM["sx4"][:, :], B["myxn3"][:, :],
                                              B["haloxn"][:, :])], SHM["f01"], 3)
            k.wait("V", k.EX, ex4)
            k.wait("T", k.EX, ex4)
            load_winT()
            emit_front(k, B, R, SHM, False)
            barrier(k)
        if PH_FFN2:
            ffn_call(False)
        barrier(k)
        emit_output(k, B, o)

        with nc.Block() as block:
            @block.sync
            def _(S):
                for fn in k.prog["S"]:
                    fn(S)

            @block.tensor
            def _(T):
                for fn in k.prog["T"]:
                    fn(T)

            @block.vector
            def _(V):
                for fn in k.prog["V"]:
                    fn(V)

            @block.scalar
            def _(A):
                for fn in k.prog["A"]:
                    fn(A)

            @block.gpsimd
            def _(G):
                for fn in k.prog["G"]:
                    fn(G)

    return nc


_NC_CACHE = None


def kernel(L0_attn_x, L0_attn_w_in, L0_attn_b_in, L0_attn_w_conv, L0_attn_b_conv,
           L0_attn_w_xproj, L0_attn_w_dt, L0_attn_b_dt, L0_attn_w_out,
           L0_ffn_router_w, L0_ffn_rms_w, L0_ffn_w1, L0_ffn_w2,
           L0_ffn_rms_w_s, L0_ffn_w1_s, L0_ffn_w2_s):
    x = np.asarray(L0_attn_x, np.float32)
    w1 = np.concatenate([np.asarray(L0_ffn_w1, np.float32),
                         np.asarray(L0_ffn_w1_s, np.float32)], 0)
    w2 = np.concatenate([np.asarray(L0_ffn_w2, np.float32),
                         np.asarray(L0_ffn_w2_s, np.float32)], 0)
    rms = np.concatenate([np.asarray(L0_ffn_rms_w, np.float32),
                          np.asarray(L0_ffn_rms_w_s, np.float32)], 0)
    rmsT = np.zeros((128, 128), np.float32)
    for e in range(NEXP):
        for c in range(NDC):
            rmsT[:, 8 * e + c] = rms[e, 128 * c:128 * (c + 1)]
    router = np.asarray(L0_ffn_router_w, np.float32)
    ident = np.eye(128, dtype=np.float32)
    w_in = np.asarray(L0_attn_w_in, np.float32)
    wind = np.ascontiguousarray(w_in[:2048].T)
    wzd = np.ascontiguousarray(w_in[2048:2064].T)
    wxp = np.asarray(L0_attn_w_xproj, np.float32)
    wdt = np.asarray(L0_attn_w_dt, np.float32)
    xpdd = np.zeros((2048, 80), np.float32)
    xpdd[:, 0:16] = wxp[0:16].T
    xpdd[:, 32:48] = wxp[16:32].T
    xpdd[:, 64:80] = wdt.T
    wcv = np.asarray(L0_attn_w_conv, np.float32)
    wconvd = np.zeros((128, 64), np.float32)
    bconvd = np.zeros((128, 16), np.float32)
    bcv = np.asarray(L0_attn_b_conv, np.float32)
    for mm in range(16):
        for dd in range(4):
            wconvd[:, 4 * mm + dd] = wcv[128 * mm:128 * (mm + 1), 0, dd]
        bconvd[:, mm] = bcv[128 * mm:128 * (mm + 1)]
    b_in = np.asarray(L0_attn_b_in, np.float32)
    binbd = np.zeros((128, 17), np.float32)
    for cc in range(16):
        binbd[:, cc] = b_in[128 * cc:128 * (cc + 1)]
    binbd[0:16, 16] = b_in[2048:2064]
    bdtd = np.zeros((128, 1), np.float32)
    bdtd[64:80, 0] = np.asarray(L0_attn_b_dt, np.float32)
    woutd = np.ascontiguousarray(np.asarray(L0_attn_w_out, np.float32)[:, :16].T)

    global _NC_CACHE
    if _NC_CACHE is None:
        _NC_CACHE = build_program()
    nc = _NC_CACHE

    in_maps = []
    for c in range(8):
        p, l = c // 2, c % 2
        b, hh = p // 2, p % 2
        own = x[b, 4096 * hh + 2048 * l: 4096 * hh + 2048 * (l + 1)]
        pfx = x[b, 2048 * l: 2048 * (l + 1)]
        in_maps.append(dict(
            xo=np.ascontiguousarray(own.T), xp=np.ascontiguousarray(pfx.T),
            w1d=w1, w2d=w2, rmsd=rmsT, routerd=router, identd=ident,
            wind=wind, wzd=wzd, xpdd=xpdd, wconvd=wconvd, bconvd=bconvd,
            binbd=binbd, bdtd=bdtd, woutd=woutd,
            laned=np.full((1, 8), l, np.int32),
            ownhmd=np.full((128, 1), float(hh) if l == 0 else 1.0, np.float32),
            pfxhmd=np.full((128, 1), 0.0 if l == 0 else 1.0, np.float32),
            hm16d=np.full((16, 1), float(hh), np.float32),
        ))
    res = run_bass_kernel_spmd(nc, in_maps, core_ids=list(range(8)))
    out = np.empty((2, 8192, 1024), np.float32)
    for c in range(8):
        p, l = c // 2, c % 2
        b, hh = p // 2, p % 2
        out[b, 4096 * hh + 2048 * l: 4096 * hh + 2048 * (l + 1)] = \
            np.asarray(res.results[c]["o"], np.float32)
    return out


# version 09
# speedup vs baseline: 1.0105x; 1.0105x over previous
import numpy as np
import ml_dtypes
from contextlib import ExitStack

import concourse.bass as bass
import concourse.mybir as mybir
from concourse.bass_utils import run_bass_kernel_spmd

F32 = mybir.dt.float32
BF16 = mybir.dt.bfloat16
I32 = mybir.dt.int32
OP = mybir.AluOpType
AF = mybir.ActivationFunctionType
AX = mybir.AxisListType.X

D = 1024
HID = 2048
NEXP = 16
NROUTED = 14
DS = 16
DIN = 2048
ZCH = DIN + DS
TOK = 2048
TT = 512
NT = TOK // TT
NDC = D // 128
NHC = HID // 128
LN_EPS = 1e-5
RMS_EPS = 1.1920929e-07
R_EPS = 1e-9

PH_MAMBA = True
PH_FFN2 = True
PH_PFX = True


class Sem:
    def __init__(self, nc, es, name):
        self.h = es.enter_context(nc.semaphore(name))
        self.n = 0

    def bump(self, k=1):
        self.n += k
        return self.n


class KB:

    def __init__(self, nc, es):
        self.nc = nc
        self.es = es
        self.prog = {"T": [], "V": [], "A": [], "G": [], "S": []}
        self.T = Sem(nc, es, "sT")
        self.V = Sem(nc, es, "sV")
        self.A = Sem(nc, es, "sA")
        self.G = Sem(nc, es, "sG")
        self.S = Sem(nc, es, "sS")
        self.GW2 = Sem(nc, es, "sGW2")
        self.MS = Sem(nc, es, "sMS")
        self.EX = Sem(nc, es, "sEX")
        self.PP = Sem(nc, es, "sPP")

    def op(self, eng, fn):
        self.prog[eng].append(fn)

    def wait(self, eng, sem, val):
        if val > 0:
            self.op(eng, lambda E, s=sem.h, v=val: E.wait_ge(s, v))


def sb(nc, es, name, shape, dtype):
    return es.enter_context(nc.sbuf_tensor(name, shape, dtype))


def psum(nc, es, name, shape):
    return es.enter_context(nc.psum_tensor(name, shape, F32))


def emit_ffn_pass1(k, B, npass):
    nc = k.nc
    row = lambda i: B["rows"][0:1, TT * i:TT * (i + 1)]
    rowb = lambda i: B["rowsb"][0:1, TT * i:TT * (i + 1)]
    t_tile_end = None
    for t in range(NT):
        if t_tile_end is not None:
            k.wait("V", k.T, t_tile_end)
        v_rb = []
        for d in range(NDC):
            k.op("V", lambda V, d=d, t=t: V.tensor_copy(
                out=B["rb"][:, TT * d:TT * (d + 1)],
                in_=B["r"][:, 2048 * d + TT * t: 2048 * d + TT * (t + 1)])
                .then_inc(k.V.h, 1))
            v_rb.append(k.V.bump())
        a_sq = []
        for d in range(NDC):
            k.wait("A", k.V, v_rb[d])
            k.op("A", lambda A, d=d: A.activation(
                out=B["sqb"][:, TT * d:TT * (d + 1)],
                in_=B["rb"][:, TT * d:TT * (d + 1)], func=AF.Square)
                .then_inc(k.A.h, 1))
            a_sq.append(k.A.bump())
        k.wait("T", k.V, v_rb[-1])
        for d in range(NDC):
            k.op("T", lambda T, d=d: T.matmul(
                out=B["pmisc"][0:1, :], lhsT=B["ones128b"][:, :],
                rhs=B["rb"][:, TT * d:TT * (d + 1)],
                start=(d == 0), stop=(d == NDC - 1))
                .then_inc(k.T.h, 1) if d == NDC - 1 else T.matmul(
                out=B["pmisc"][0:1, :], lhsT=B["ones128b"][:, :],
                rhs=B["rb"][:, TT * d:TT * (d + 1)],
                start=(d == 0), stop=(d == NDC - 1)))
        t_sumx = k.T.bump()
        k.wait("T", k.A, a_sq[-1])
        for d in range(NDC):
            last = d == NDC - 1
            k.op("T", lambda T, d=d, last=last: (
                T.matmul(out=B["pmisc2"][0:1, :], lhsT=B["ones128b"][:, :],
                         rhs=B["sqb"][:, TT * d:TT * (d + 1)],
                         start=(d == 0), stop=last).then_inc(k.T.h, 1)
                if last else
                T.matmul(out=B["pmisc2"][0:1, :], lhsT=B["ones128b"][:, :],
                         rhs=B["sqb"][:, TT * d:TT * (d + 1)],
                         start=(d == 0), stop=last)))
        t_sumsq = k.T.bump()
        k.wait("V", k.T, t_sumsq)
        def rowmath(V):
            V.tensor_scalar(out=row(0), in0=B["pmisc"][0:1, :],
                            scalar1=1.0 / D, scalar2=None, op0=OP.mult)
            V.tensor_scalar(out=row(1), in0=B["pmisc2"][0:1, :],
                            scalar1=1.0 / D, scalar2=None, op0=OP.mult)
            V.tensor_tensor(out=row(2), in0=row(0), in1=row(0), op=OP.mult)
            V.tensor_tensor(out=row(1), in0=row(1), in1=row(2),
                            op=OP.subtract).then_inc(k.V.h, 1)
        k.op("V", rowmath)
        v_var = k.V.bump()
        k.wait("A", k.V, v_var)
        k.op("A", lambda A: A.activation(
            out=row(2), in_=row(1), func=AF.Sqrt,
            bias=B["lneps"][0:1, 0:1]).then_inc(k.A.h, 1))
        a_sq1 = k.A.bump()
        k.wait("V", k.A, a_sq1)
        def rowmath2(V):
            V.reciprocal(out=row(3), in_=row(2))
            V.drain()
            V.tensor_tensor(out=row(2), in0=row(1), in1=row(3), op=OP.mult)
            V.tensor_tensor(out=row(2), in0=row(2), in1=row(3),
                            op=OP.mult).then_inc(k.V.h, 1)
        k.op("V", rowmath2)
        v_ms = k.V.bump()
        k.wait("A", k.V, v_ms)
        k.op("A", lambda A: A.activation(
            out=row(4), in_=row(2), func=AF.Sqrt,
            bias=B["rmseps"][0:1, 0:1]).then_inc(k.A.h, 1))
        a_sq2 = k.A.bump()
        k.wait("V", k.A, a_sq2)
        def rowmath3(V):
            V.reciprocal(out=row(4), in_=row(4))
            V.drain()
            V.tensor_tensor(out=row(4), in0=row(3), in1=row(4), op=OP.mult)
            V.tensor_copy(out=rowb(0), in_=row(4))
            V.tensor_tensor(out=row(2), in0=row(0), in1=row(4), op=OP.mult)
            V.tensor_scalar(out=row(2), in0=row(2),
                            scalar1=-1.0, scalar2=None, op0=OP.mult)
            V.tensor_copy(out=rowb(1), in_=row(2))
            V.tensor_copy(out=rowb(2), in_=row(0)).then_inc(k.V.h, 1)
        k.op("V", rowmath3)
        v_rows = k.V.bump()

        k.wait("T", k.V, v_rows)
        k.op("T", lambda T: T.matmul(out=B["pmisc"][:, :], lhsT=B["onesb"][0:1, :],
                                     rhs=rowb(0), start=True, stop=True)
             .then_inc(k.T.h, 1))
        t_cg = k.T.bump()
        k.op("T", lambda T: T.matmul(out=B["pmisc2"][:, :], lhsT=B["onesb"][0:1, :],
                                     rhs=rowb(1), start=True, stop=True)
             .then_inc(k.T.h, 1))
        t_bg = k.T.bump()
        k.wait("V", k.T, t_bg)
        def cgbg(V):
            V.tensor_copy(out=B["CgS"][:, :], in_=B["pmisc"][:, :])
            V.tensor_copy(out=B["BgS"][:, :], in_=B["pmisc2"][:, :]).then_inc(k.V.h, 1)
            V.drain()
        k.op("V", cgbg)
        v_cgbg = k.V.bump()

        for d in range(NDC):
            last = d == NDC - 1
            def gbuild(V, d=d, t=t, last=last):
                V.tensor_tensor(out=B["gt"][:, 0:TT],
                                in0=B["rb"][:, TT * d:TT * (d + 1)],
                                in1=B["CgS"][:, :], op=OP.mult)
                ins = V.tensor_tensor(out=B["g"][:, 2048 * d + TT * t:2048 * d + TT * (t + 1)],
                                      in0=B["gt"][:, 0:TT],
                                      in1=B["BgS"][:, :], op=OP.add)
                if last:
                    ins.then_inc(k.V.h, 1)
            k.op("V", gbuild)
        v_g = k.V.bump()

        k.wait("T", k.V, v_g)
        for d in range(NDC):
            k.op("T", lambda T, d=d: T.matmul(
                out=B["pmisc"][0:14, :], lhsT=B["routerb"][:, 14 * d:14 * (d + 1)],
                rhs=B["rb"][:, TT * d:TT * (d + 1)], start=(d == 0), stop=False))
        k.op("T", lambda T: T.matmul(
            out=B["pmisc"][0:14, :], lhsT=B["negcs"][0:1, :],
            rhs=rowb(2), start=False, stop=True).then_inc(k.T.h, 1))
        t_logit = k.T.bump()
        k.wait("V", k.T, t_logit)
        k.op("V", lambda V: V.tensor_copy(out=B["lgsb"][0:14, 0:TT],
                                          in_=B["pmisc"][0:14, :]).then_inc(k.V.h, 1))
        v_lg = k.V.bump()

        for j in range(4):
            k.wait("T", k.V, v_lg)
            k.op("T", lambda T, j=j: (
                T.transpose(out=B["pmisc2"][:, 15 * j:15 * j + 14],
                            in_=B["lgsb"][0:14, 128 * j:128 * (j + 1)],
                            identity=B["ident"][0:14, 0:14]),
                T.transpose(out=B["pmisc2"][:, 15 * j + 14:15 * j + 15],
                            in_=B["rows"][0:1, TT * 3 + 128 * j:TT * 3 + 128 * (j + 1)],
                            identity=B["ident"][0:1, 0:1]).then_inc(k.T.h, 1)))
            t_tr = k.T.bump()
            k.wait("V", k.T, t_tr)

            def tmmath(V, j=j):
                tm = B["tm"]
                V.tensor_copy(out=tm[:, 14:15], in_=B["pmisc2"][:, 15 * j + 14:15 * j + 15])
                V.drain()
                V.tensor_scalar(out=tm[:, 0:14], in0=B["pmisc2"][:, 15 * j:15 * j + 14],
                                scalar1=tm[:, 14:15],
                                scalar2=None, op0=OP.mult).then_inc(k.V.h, 1)
            k.op("V", tmmath)
            v_tm = k.V.bump()
            k.wait("A", k.V, v_tm)
            k.op("A", lambda A: A.activation(
                out=B["tm"][:, 16:30], in_=B["tm"][:, 0:14], func=AF.Exp,
                accum_out=B["tm"][:, 30:31]).then_inc(k.A.h, 1))
            a_exp = k.A.bump()
            k.wait("V", k.A, a_exp)

            def top2(V, j=j, t=t):
                tm = B["tm"]
                V.reciprocal(out=tm[:, 31:32], in_=tm[:, 30:31])
                V.drain()
                V.tensor_scalar(out=tm[:, 32:46], in0=tm[:, 16:30],
                                scalar1=tm[:, 31:32], scalar2=None, op0=OP.mult)
                V.drain()
                V.reduce_max(out=tm[:, 46:47], in_=tm[:, 32:46], axis=AX)
                V.drain()
                V.tensor_copy(out=B["vt8"][:, 0:1], in_=tm[:, 46:47])
                V.drain()
                V.match_replace(out=tm[:, 48:62], in_to_replace=B["vt8"][:, :],
                                in_values=tm[:, 32:46], imm_value=-1e30)
                V.drain()
                V.reduce_max(out=tm[:, 47:48], in_=tm[:, 48:62], axis=AX)
                V.drain()
                V.tensor_scalar(out=tm[:, 48:62], in0=tm[:, 32:46],
                                scalar1=tm[:, 47:48], scalar2=None, op0=OP.is_ge)
                V.drain()
                V.tensor_tensor(out=tm[:, 48:62], in0=tm[:, 32:46],
                                in1=tm[:, 48:62], op=OP.mult)
                V.drain()
                V.reduce_sum(out=tm[:, 62:63], in_=tm[:, 48:62], axis=AX)
                V.drain()
                V.tensor_scalar(out=tm[:, 62:63], in0=tm[:, 62:63],
                                scalar1=R_EPS, scalar2=2.0, op0=OP.add, op1=OP.mult)
                V.drain()
                V.reciprocal(out=tm[:, 63:64], in_=tm[:, 62:63])
                V.drain()
                V.memset(B["mtm"][:, 16 * j:16 * (j + 1)], 0.0)
                V.drain()
                V.tensor_scalar(out=B["mtm"][:, 16 * j:16 * j + 14], in0=tm[:, 48:62],
                                scalar1=tm[:, 63:64], scalar2=None,
                                op0=OP.mult).then_inc(k.V.h, 1)
            k.op("V", top2)
            v_m = k.V.bump()
            k.wait("T", k.V, v_m)
            k.op("T", lambda T, j=j: T.transpose(
                out=B["pmisc"][0:16, 128 * j:128 * (j + 1)],
                in_=B["mtm"][:, 16 * j:16 * (j + 1)],
                identity=B["ident"][:, :]).then_inc(k.T.h, 1))
            t_mb = k.T.bump()
            k.wait("V", k.T, t_mb)
            k.op("V", lambda V, j=j, t=t: V.tensor_copy(
                out=B["m_fm"][0:16, TT * t + 128 * j:TT * t + 128 * (j + 1)],
                in_=B["pmisc"][0:16, 128 * j:128 * (j + 1)]).then_inc(k.V.h, 1))
            k.V.bump()
        t_tile_end = k.T.n
    return k.V.n, k.T.n, k.A.n


def emit_ffn_pass2(k, B, W1dram, W2dram, wctr):
    nc = k.nc
    for e in range(NEXP):
        if wctr["t_w1_done"] is not None:
            k.wait("G", k.T, wctr["t_w1_done"])
        g_w1h = None
        for half in range(2):
            k.op("G", lambda G, e=e, half=half: G.dma_start(
                out=B["w1s"][:, :].rearrange("p (c h) -> p c h", c=8)[:, 4 * half:4 * (half + 1), :],
                in_=W1dram[e].rearrange("(c p) h -> p c h", p=128)[:, 4 * half:4 * (half + 1), :])
                .then_inc(k.G.h, 16))
            if half == 0:
                g_w1h = k.G.bump(16)
            else:
                k.G.bump(16)
        g_w1 = k.G.n
        if wctr["t_w2_done"] is not None:
            k.wait("G", k.T, wctr["t_w2_done"])
        for half in range(2):
            k.op("G", lambda G, e=e, half=half: G.dma_start(
                out=B["w2s"][:, :].rearrange("p (c h) -> p c h", c=16)[:, 8 * half:8 * (half + 1), :],
                in_=W2dram[e].rearrange("(c p) h -> p c h", p=128)[:, 8 * half:8 * (half + 1), :])
                .then_inc(k.GW2.h, 16))
            k.GW2.bump(16)
        g_w2 = k.GW2.n

        k.wait("V", k.G, g_w1h)
        for c in range(4):
            last = c == 3
            def w1scaleA(V, e=e, c=c, last=last):
                ins = V.tensor_scalar(out=B["w1s"][:, 2048 * c:2048 * (c + 1)],
                                      in0=B["w1s"][:, 2048 * c:2048 * (c + 1)],
                                      scalar1=B["rms"][:, 8 * e + c:8 * e + c + 1],
                                      scalar2=None, op0=OP.mult)
                if last:
                    ins.then_inc(k.V.h, 1)
            k.op("V", w1scaleA)
        v_w1h = k.V.bump()
        k.wait("V", k.G, g_w1)
        for c in range(4, NDC):
            last = c == NDC - 1
            def w1scale(V, e=e, c=c, last=last):
                ins = V.tensor_scalar(out=B["w1s"][:, 2048 * c:2048 * (c + 1)],
                                      in0=B["w1s"][:, 2048 * c:2048 * (c + 1)],
                                      scalar1=B["rms"][:, 8 * e + c:8 * e + c + 1],
                                      scalar2=None, op0=OP.mult)
                if last:
                    ins.then_inc(k.V.h, 1)
            k.op("V", w1scale)
        v_w1 = k.V.bump()

        for t in range(NT):
            shared = e >= NROUTED
            if not shared:
                builds = [t] if t == 0 else []
                if t < NT - 1:
                    builds.append(t + 1)
                for tb in builds:
                    sl = tb % 2
                    prev = wctr["t_mrow"][sl]
                    if prev is not None:
                        k.wait("S", k.T, prev)
                    k.wait("S", k.V, wctr["v_g_ready"])
                    k.op("S", lambda S, e=e, tb=tb, sl=sl: S.dma_start(
                        out=B["mrow"][0:1, TT * sl:TT * (sl + 1)],
                        in_=B["m_fm"][e:e + 1, TT * tb:TT * (tb + 1)]).then_inc(k.S.h, 16))
                    s_mrow = k.S.bump(16)
                    if wctr["v_ce"][sl] is not None:
                        k.wait("T", k.V, wctr["v_ce"][sl])
                    k.wait("T", k.S, s_mrow)
                    k.op("T", lambda T, sl=sl: T.matmul(
                        out=B["pmisc"][0:128, 256 * sl:256 * (sl + 1)],
                        lhsT=B["onesb"][0:1, :],
                        rhs=B["mrow"][0:1, TT * sl:TT * sl + 256],
                        start=True, stop=False))
                    k.op("T", lambda T, sl=sl: T.matmul(
                        out=B["pmisc2"][0:128, 256 * sl:256 * (sl + 1)],
                        lhsT=B["onesb"][0:1, :],
                        rhs=B["mrow"][0:1, TT * sl + 256:TT * (sl + 1)],
                        start=True, stop=True).then_inc(k.T.h, 1))
                    t_ce = k.T.bump()
                    wctr["t_mrow"][sl] = t_ce
                    k.wait("V", k.T, t_ce)
                    def ce_evac(V, sl=sl):
                        V.tensor_copy(out=B["Ce"][:, TT * sl:TT * sl + 256],
                                      in_=B["pmisc"][0:128, 256 * sl:256 * (sl + 1)])
                        V.tensor_copy(out=B["Ce"][:, TT * sl + 256:TT * (sl + 1)],
                                      in_=B["pmisc2"][0:128, 256 * sl:256 * (sl + 1)]).then_inc(k.V.h, 1)
                        V.drain()
                    k.op("V", ce_evac)
                    wctr["v_ce"][sl] = k.V.bump()

            k.wait("T", k.V, v_w1)
            if t == 0 and e == 0:
                k.wait("T", k.V, wctr["v_g_ready"])
            a_silu = []
            for m in range(NHC):
                pw = B["pw1a"] if m % 2 == 0 else B["pw1b"]
                if m >= 2:
                    k.wait("T", k.A, a_silu[m - 2])
                elif wctr.get("a_silu_tail"):
                    k.wait("T", k.A, wctr["a_silu_tail"][m])
                for c in range(NDC):
                    last = c == NDC - 1
                    def w1mm(T, e=e, t=t, m=m, c=c, last=last, pw=pw):
                        ins = T.matmul(
                            out=pw[:, :],
                            lhsT=B["w1s"][:, 2048 * c + 128 * m:2048 * c + 128 * (m + 1)],
                            rhs=B["g"][:, 2048 * c + TT * t:2048 * c + TT * (t + 1)],
                            start=(c == 0), stop=last)
                        if last:
                            ins.then_inc(k.T.h, 1)
                    k.op("T", w1mm)
                t_m1 = k.T.bump()
                k.wait("A", k.T, t_m1)
                k.op("A", lambda A, m=m, pw=pw: A.activation(
                    out=B["h"][:, TT * m:TT * (m + 1)], in_=pw[:, :],
                    func=AF.Silu).then_inc(k.A.h, 1))
                a_silu.append(k.A.bump())
            wctr["a_silu_tail"] = [a_silu[-2], a_silu[-1]]
            wctr["t_w1_done"] = k.T.n
            v_hm = []
            for m in range(NHC):
                k.wait("V", k.A, a_silu[m])
                if shared:
                    k.op("V", lambda V, m=m: V.tensor_scalar(
                        out=B["h"][:, TT * m:TT * (m + 1)],
                        in0=B["h"][:, TT * m:TT * (m + 1)],
                        scalar1=0.5, scalar2=None, op0=OP.mult).then_inc(k.V.h, 1))
                else:
                    k.op("V", lambda V, m=m, t=t: V.tensor_tensor(
                        out=B["h"][:, TT * m:TT * (m + 1)],
                        in0=B["h"][:, TT * m:TT * (m + 1)],
                        in1=B["Ce"][:, TT * (t % 2):TT * (t % 2 + 1)],
                        op=OP.mult).then_inc(k.V.h, 1))
                v_hm.append(k.V.bump())
            k.wait("T", k.GW2, g_w2)
            for j in range(4):
                pw = B["pw2a"] if j % 2 == 0 else B["pw2b"]
                if wctr["v_ev"][j % 2] is not None:
                    k.wait("T", k.V, wctr["v_ev"][j % 2])
                for m in range(NHC):
                    k.wait("T", k.V, v_hm[m])
                    for dd in range(2):
                        dmi = 2 * j + dd
                        last = (m == NHC - 1) and (dd == 1)
                        def w2mm(T, e=e, t=t, m=m, dmi=dmi, dd=dd, last=last, pw=pw):
                            ins = T.matmul(
                                out=pw[:, 512 * dd:512 * (dd + 1)],
                                lhsT=B["w2s"][:, 1024 * m + 128 * dmi:1024 * m + 128 * (dmi + 1)],
                                rhs=B["h"][:, TT * m:TT * (m + 1)],
                                start=(m == 0), stop=(m == NHC - 1))
                            if last:
                                ins.then_inc(k.T.h, 1)
                        k.op("T", w2mm)
                t_m2 = k.T.bump()
                k.wait("V", k.T, t_m2)
                def evac(V, j=j, t=t, pw=pw):
                    rsl = B["r"][:, :].rearrange("p (d x) -> p d x", d=NDC)[
                        :, 2 * j:2 * (j + 1), TT * t:TT * (t + 1)]
                    psl = pw[:, :].rearrange("p (d x) -> p d x", d=2)
                    V.tensor_tensor(out=rsl, in0=rsl, in1=psl, op=OP.add).then_inc(k.V.h, 1)
                k.op("V", evac)
                wctr["v_ev"][j % 2] = k.V.bump()
            wctr["t_w2_done"] = k.T.n
    return


def emit_output(k, B, o_dram):
    st_hist = []
    ev_hist = []
    for j in range(16):
        par = j % 2
        pm = B["pmisc"] if par == 0 else B["pmisc2"]
        v_e = None
        for grp in range(2):
            prev = 2 * (j - 2) + grp if grp == 0 else 2 * j + grp - 1
            if grp == 1:
                k.wait("T", k.V, ev_hist[2 * j])
            elif j >= 2:
                k.wait("T", k.V, ev_hist[2 * (j - 2) + 1])
            for dd in range(4):
                d = 4 * grp + dd
                def tr(T, j=j, d=d, dd=dd, pm=pm, last=(dd == 3)):
                    ins = T.transpose(out=pm[:, 128 * dd:128 * (dd + 1)],
                                      in_=B["r"][:, 2048 * d + 128 * j:2048 * d + 128 * (j + 1)],
                                      identity=B["ident"][:, :])
                    if last:
                        ins.then_inc(k.T.h, 1)
                k.op("T", tr)
            t_tr = k.T.bump()
            k.wait("V", k.T, t_tr)
            if grp == 0 and j >= 2:
                k.wait("V", k.S, st_hist[-1])
            def ev(V, grp=grp, pm=pm, par=par):
                V.tensor_copy(out=B["outT"][:, 1024 * par + 512 * grp:1024 * par + 512 * (grp + 1)],
                              in_=pm[:, :]).then_inc(k.V.h, 1)
            k.op("V", ev)
            ev_hist.append(k.V.bump())
            v_e = ev_hist[-1]
        k.wait("S", k.V, v_e)
        k.op("S", lambda S, j=j, par=par: S.dma_start(
            out=o_dram[128 * j:128 * (j + 1), :],
            in_=B["outT"][:, 1024 * par:1024 * (par + 1)]).then_inc(k.S.h, 16))
        st_hist.append(k.S.bump(16))
    k.wait("S", k.S, k.S.n)


def barrier(k):
    sems = [("T", k.T), ("V", k.V), ("A", k.A), ("G", k.G), ("S", k.S),
            ("EX", k.EX), ("GW2", k.GW2)]
    for ename in ["T", "V", "A", "G", "S"]:
        for sname, sem in sems:
            if sname != ename and sem.n > 0:
                k.wait(ename, sem, sem.n)


def emit_ln_pass(k, B):
    row = lambda i: B["rows"][0:1, TT * i:TT * (i + 1)]
    rowb = lambda i: B["rowsb"][0:1, TT * i:TT * (i + 1)]
    t_tile_end = None
    for t in range(NT):
        if t_tile_end is not None:
            k.wait("V", k.T, t_tile_end)
        v_rb = []
        for d in range(NDC):
            k.op("V", lambda V, d=d, t=t: V.tensor_copy(
                out=B["rb"][:, TT * d:TT * (d + 1)],
                in_=B["r"][:, 2048 * d + TT * t: 2048 * d + TT * (t + 1)])
                .then_inc(k.V.h, 1))
            v_rb.append(k.V.bump())
        a_sq = []
        for d in range(NDC):
            k.wait("A", k.V, v_rb[d])
            k.op("A", lambda A, d=d: A.activation(
                out=B["sqb"][:, TT * d:TT * (d + 1)],
                in_=B["rb"][:, TT * d:TT * (d + 1)], func=AF.Square)
                .then_inc(k.A.h, 1))
            a_sq.append(k.A.bump())
        k.wait("T", k.V, v_rb[-1])
        for d in range(NDC):
            last = d == NDC - 1
            k.op("T", lambda T, d=d, last=last: (
                T.matmul(out=B["pmisc"][0:1, :], lhsT=B["ones128b"][:, :],
                         rhs=B["rb"][:, TT * d:TT * (d + 1)],
                         start=(d == 0), stop=last).then_inc(k.T.h, 1)
                if last else
                T.matmul(out=B["pmisc"][0:1, :], lhsT=B["ones128b"][:, :],
                         rhs=B["rb"][:, TT * d:TT * (d + 1)],
                         start=(d == 0), stop=last)))
        k.T.bump()
        k.wait("T", k.A, a_sq[-1])
        for d in range(NDC):
            last = d == NDC - 1
            k.op("T", lambda T, d=d, last=last: (
                T.matmul(out=B["pmisc2"][0:1, :], lhsT=B["ones128b"][:, :],
                         rhs=B["sqb"][:, TT * d:TT * (d + 1)],
                         start=(d == 0), stop=last).then_inc(k.T.h, 1)
                if last else
                T.matmul(out=B["pmisc2"][0:1, :], lhsT=B["ones128b"][:, :],
                         rhs=B["sqb"][:, TT * d:TT * (d + 1)],
                         start=(d == 0), stop=last)))
        t_sumsq = k.T.bump()
        k.wait("V", k.T, t_sumsq)
        def rowmath(V):
            V.tensor_scalar(out=row(0), in0=B["pmisc"][0:1, :],
                            scalar1=1.0 / D, scalar2=None, op0=OP.mult)
            V.tensor_scalar(out=row(1), in0=B["pmisc2"][0:1, :],
                            scalar1=1.0 / D, scalar2=None, op0=OP.mult)
            V.tensor_tensor(out=row(2), in0=row(0), in1=row(0), op=OP.mult)
            V.tensor_tensor(out=row(1), in0=row(1), in1=row(2),
                            op=OP.subtract).then_inc(k.V.h, 1)
        k.op("V", rowmath)
        v_var = k.V.bump()
        k.wait("A", k.V, v_var)
        k.op("A", lambda A: A.activation(
            out=row(2), in_=row(1), func=AF.Sqrt,
            bias=B["lneps"][0:1, 0:1]).then_inc(k.A.h, 1))
        a_s1 = k.A.bump()
        k.wait("V", k.A, a_s1)
        def rowmath2(V):
            V.reciprocal(out=row(3), in_=row(2))
            V.drain()
            V.tensor_copy(out=rowb(0), in_=row(3))
            V.tensor_tensor(out=row(2), in0=row(0), in1=row(3), op=OP.mult)
            V.tensor_scalar(out=row(2), in0=row(2), scalar1=-1.0,
                            scalar2=None, op0=OP.mult)
            V.tensor_copy(out=rowb(1), in_=row(2)).then_inc(k.V.h, 1)
        k.op("V", rowmath2)
        v_rows = k.V.bump()
        k.wait("T", k.V, v_rows)
        k.op("T", lambda T: T.matmul(out=B["pmisc"][:, :], lhsT=B["onesb"][0:1, :],
                                     rhs=rowb(0), start=True, stop=True)
             .then_inc(k.T.h, 1))
        k.T.bump()
        k.op("T", lambda T: T.matmul(out=B["pmisc2"][:, :], lhsT=B["onesb"][0:1, :],
                                     rhs=rowb(1), start=True, stop=True)
             .then_inc(k.T.h, 1))
        t_bg = k.T.bump()
        k.wait("V", k.T, t_bg)
        def cgbg(V):
            V.tensor_copy(out=B["CgS"][:, :], in_=B["pmisc"][:, :])
            V.tensor_copy(out=B["BgS"][:, :], in_=B["pmisc2"][:, :]).then_inc(k.V.h, 1)
            V.drain()
        k.op("V", cgbg)
        v_cb = k.V.bump()
        for d in range(NDC):
            last = d == NDC - 1
            def gbuild(V, d=d, t=t, last=last):
                V.tensor_tensor(out=B["gt"][:, 0:TT],
                                in0=B["rb"][:, TT * d:TT * (d + 1)],
                                in1=B["CgS"][:, :], op=OP.mult)
                ins = V.tensor_tensor(
                    out=B["g"][:, 2048 * d + TT * t:2048 * d + TT * (t + 1)],
                    in0=B["gt"][:, 0:TT], in1=B["BgS"][:, :], op=OP.add)
                if last:
                    ins.then_inc(k.V.h, 1)
            k.op("V", gbuild)
        k.V.bump()
        t_tile_end = k.T.n
    def myext(V):
        for d in range(NDC):
            ins = V.tensor_copy(out=B["myxn3"][:, 3 * d:3 * (d + 1)],
                                in_=B["g"][:, 2048 * d + 2045:2048 * d + 2048])
        ins.then_inc(k.V.h, 1)
    k.op("V", myext)
    return k.V.bump()


def g_poll(G, R, k, flag_dram, expect):
    G.reg_mov(R["rdone"], 1)
    with G.While(R["rdone"]):
        G.reg_add(R["rcnt"], R["rcnt"], 16)
        G.dma_start(out=R["fv"][:, :], in_=flag_dram[:, :]).then_inc(k.PP.h, 16)
        G.wait_ge(k.PP.h, R["rcnt"])
        G.load(R["rflag"], R["fv"][0:1, 0:1])
        G.reg_alu(R["rdone"], R["rflag"], expect, OP.is_lt)


def emit_exchange(k, B, R, sender_lane, pairs, flag_dram, flag_val):
    n = len(pairs)
    base = k.EX.n

    msv = k.MS.bump()

    def go(G):
        with G.If_eq(R["rlane"], sender_lane):
            cnt = base
            for (slot, src, dst) in pairs:
                G.dma_start(out=slot, in_=src).then_inc(k.EX.h, 16)
                cnt += 16
                G.wait_ge(k.EX.h, cnt)
            G.memset(R["fl"][:, :], flag_val).then_inc(k.MS.h, 1)
            G.wait_ge(k.MS.h, msv)
            G.dma_start(out=flag_dram[:, :], in_=R["fl"][:, :]).then_inc(k.EX.h, 16)
            G.wait_ge(k.EX.h, base + 16 * (n + 1))
        with G.Else():
            G.memset(R["fv"][:, :], 0).then_inc(k.MS.h, 1)
            G.wait_ge(k.MS.h, msv)
            g_poll(G, R, k, flag_dram, flag_val)
            cnt = base
            for (slot, src, dst) in pairs:
                G.dma_start(out=dst, in_=slot).then_inc(k.EX.h, 16)
                cnt += 16
                G.wait_ge(k.EX.h, cnt)
            G.dma_start(out=R["fv"][:, :], in_=flag_dram[:, :]).then_inc(k.EX.h, 16)
            G.wait_ge(k.EX.h, base + 16 * (n + 1))
    k.op("G", go)
    return k.EX.bump(16 * (n + 1))


def emit_front(k, B, R, SHM, is_pfx):
    w2f = B["w2s"][:, :].bitcast(F32)
    Asb = w2f[:, 0:2048]
    Bxsb = w2f[:, 2048:4096]
    hsb = w2f[:, 4096:6144]
    z16sb = w2f[:, 6144:8192]
    xcpre = B["h"][:, 0:2051]
    xcraw = B["h"][:, 2560:4608]
    xcs = B["h"][:, 4608:6656]
    cb = B["carrybuf"]
    hmkey = "pfxhm" if is_pfx else "ownhm"

    for q in range(NT):
        for c in range(NDC):
            last = c == NDC - 1
            def zmm(T, q=q, c=c, last=last):
                ins = T.matmul(out=B["pmisc"][0:16, :],
                               lhsT=B["wzT"][:, 16 * c:16 * (c + 1)],
                               rhs=B["g"][:, 2048 * c + TT * q:2048 * c + TT * (q + 1)],
                               start=(c == 0), stop=last)
                if last:
                    ins.then_inc(k.T.h, 1)
            k.op("T", zmm)
        t_z = k.T.bump()
        k.wait("A", k.T, t_z)
        k.op("A", lambda A, q=q: A.activation(
            out=z16sb[0:16, TT * q:TT * (q + 1)], in_=B["pmisc"][0:16, :],
            func=AF.Identity, bias=B["binb"][0:16, 16:17]).then_inc(k.A.h, 1))
        a_z = k.A.bump()
        k.wait("T", k.A, a_z)

    a_evac = [None, None]
    v_conv_last = None
    for m in range(NHC):
        for q in range(NT):
            pw = B["pw1a"] if (m * NT + q) % 2 == 0 else B["pw1b"]
            par = (m * NT + q) % 2
            if a_evac[par] is not None:
                k.wait("T", k.A, a_evac[par])
            for c in range(NDC):
                last = c == NDC - 1
                def wmm(T, m=m, q=q, c=c, last=last, pw=pw):
                    ins = T.matmul(
                        out=pw[:, :],
                        lhsT=B["w1s"][:, 2048 * c + 128 * m:2048 * c + 128 * (m + 1)],
                        rhs=B["g"][:, 2048 * c + TT * q:2048 * c + TT * (q + 1)],
                        start=(c == 0), stop=last)
                    if last:
                        ins.then_inc(k.T.h, 1)
                k.op("T", wmm)
            t_mq = k.T.bump()
            k.wait("A", k.T, t_mq)
            k.op("A", lambda A, m=m, q=q, pw=pw: A.activation(
                out=xcpre[:, 3 + TT * q:3 + TT * (q + 1)], in_=pw[:, :],
                func=AF.Identity, bias=B["binb"][:, m:m + 1]).then_inc(k.A.h, 1))
            a_evac[par] = k.A.bump()
        for c in range(NDC):
            last = c == NDC - 1
            def hmm(T, m=m, c=c, last=last):
                ins = T.matmul(out=B["pmisc2"][:, 0:3],
                               lhsT=B["w1s"][:, 2048 * c + 128 * m:2048 * c + 128 * (m + 1)],
                               rhs=B["haloxn"][:, 3 * c:3 * (c + 1)],
                               start=(c == 0), stop=last)
                if last:
                    ins.then_inc(k.T.h, 1)
            k.op("T", hmm)
        t_h = k.T.bump()
        k.wait("A", k.T, t_h)
        k.op("A", lambda A, m=m: A.activation(
            out=xcpre[:, 0:3], in_=B["pmisc2"][:, 0:3],
            func=AF.Identity, bias=B["binb"][:, m:m + 1]).then_inc(k.A.h, 1))
        a_h = k.A.bump()
        k.wait("V", k.A, max(a_h, a_evac[0] or 0, a_evac[1] or 0))
        def conv(V, m=m):
            V.tensor_scalar(out=xcpre[:, 0:3], in0=xcpre[:, 0:3],
                            scalar1=B[hmkey][:, 0:1], scalar2=None, op0=OP.mult)
            V.drain()
            V.tensor_scalar(out=xcraw[:, :], in0=xcpre[:, 0:2048],
                            scalar1=B["wconv"][:, 4 * m:4 * m + 1],
                            scalar2=None, op0=OP.mult)
            V.drain()
            for dd in range(1, 4):
                V.scalar_tensor_tensor(out=xcraw[:, :], in0=xcpre[:, dd:2048 + dd],
                                       scalar=B["wconv"][:, 4 * m + dd:4 * m + dd + 1],
                                       in1=xcraw[:, :], op0=OP.mult, op1=OP.add)
                V.drain()
            V.tensor_copy(out=xcraw[:, 0:1], in_=xcraw[:, 0:1]).then_inc(k.V.h, 1)
        k.op("V", conv)
        v_conv = k.V.bump()
        k.wait("A", k.V, v_conv)
        k.op("A", lambda A, m=m: A.activation(
            out=xcs[:, :], in_=xcraw[:, :], func=AF.Silu,
            bias=B["bconv"][:, m:m + 1]).then_inc(k.A.h, 1))
        a_silu = k.A.bump()
        if m == 0:
            k.op("A", lambda A: A.activation(
                out=B["m_fm"][0:16, :], in_=xcraw[0:16, :], func=AF.Silu,
                bias=B["bconv"][0:16, 0:1]).then_inc(k.A.h, 1))
            k.A.bump()
        k.wait("T", k.A, a_silu)
        for q in range(NT):
            pw = B["pw2a"] if q < 2 else B["pw2b"]
            qq = q % 2
            last = m == NHC - 1
            qlast = q == NT - 1
            def xpmm(T, m=m, q=q, qq=qq, pw=pw, last=last, qlast=qlast):
                ins = T.matmul(out=pw[0:80, TT * qq:TT * (qq + 1)],
                               lhsT=B["xpd"][:, 80 * m:80 * (m + 1)],
                               rhs=xcs[:, TT * q:TT * (q + 1)],
                               start=(m == 0), stop=last)
                if qlast:
                    ins.then_inc(k.T.h, 1)
            k.op("T", xpmm)
            if qlast:
                t_xp_last = k.T.bump()
        v_conv_last = v_conv
    t_xpd = k.T.n

    for q in range(NT):
        pw = B["pw2a"] if q < 2 else B["pw2b"]
        qq = q % 2
        P = pw[:, TT * qq:TT * (qq + 1)]
        sA = B["s16"][0:16, 0:TT]
        sB = B["s16"][0:16, TT:2 * TT]
        sC = B["s16"][0:16, 2 * TT:3 * TT]
        k.wait("A", k.T, t_xpd)
        if q > 0:
            k.wait("A", k.V, k.V.n)
        def ab_a1(A, P=P, sA=sA, sB=sB, sC=sC):
            A.activation(out=sA, in_=P[64:80, :], func=AF.Identity, bias=B["bdt"][64:80, 0:1])
            A.activation(out=sB, in_=P[0:16, :], func=AF.Abs)
            A.activation(out=sC, in_=sA, func=AF.Abs)
            A.activation(out=sC, in_=sC, func=AF.Exp, scale=-1.0)
            A.activation(out=sC, in_=sC, func=AF.Ln,
                         bias=B["one16"][0:16, 0:1]).then_inc(k.A.h, 1)
        k.op("A", ab_a1)
        a_1 = k.A.bump()
        k.wait("V", k.A, a_1)
        def ab_v1(V, q=q, P=P, sA=sA, sB=sB, sC=sC):
            V.tensor_scalar(out=sA, in0=sA, scalar1=0.0, scalar2=None, op0=OP.max)
            V.drain()
            V.tensor_tensor(out=sA, in0=sA, in1=sC, op=OP.add)
            V.drain()
            V.tensor_tensor(out=sB, in0=sB, in1=sA, op=OP.mult)
            V.drain()
            V.tensor_copy(out=sC, in_=P[32:48, :])
            V.drain()
            V.tensor_tensor(out=sC, in0=sC, in1=sA, op=OP.mult)
            V.drain()
            V.tensor_tensor(out=Bxsb[0:16, TT * q:TT * (q + 1)], in0=sC,
                            in1=B["m_fm"][0:16, TT * q:TT * (q + 1)],
                            op=OP.mult).then_inc(k.V.h, 1)
        k.op("V", ab_v1)
        v_1 = k.V.bump()
        k.wait("A", k.V, v_1)
        k.op("A", lambda A, q=q, sB=sB: A.activation(
            out=Asb[0:16, TT * q:TT * (q + 1)], in_=sB, func=AF.Exp,
            scale=-1.0).then_inc(k.A.h, 1))
        k.A.bump()

    k.wait("V", k.A, k.A.n)
    if is_pfx:
        k.op("V", lambda V: V.memset(cb[0:16, 8:9], 0.0).then_inc(k.V.h, 1))
        k.V.bump()
    def scan1(V):
        V.drain()
        V.tensor_tensor_scan(out=hsb[0:16, :], data0=Asb[0:16, :],
                             data1=Bxsb[0:16, :], initial=cb[0:16, 8:9],
                             op0=OP.mult, op1=OP.add)
        V.drain()
        V.tensor_copy(out=cb[0:16, 0:1], in_=hsb[0:16, 2047:2048]).then_inc(k.V.h, 1)
        V.drain()
    k.op("V", scan1)
    v_s1 = k.V.bump()
    k.wait("G", k.V, v_s1)
    if is_pfx:
        ex = emit_exchange(k, B, R, 0, [(SHM["c16a"][0:16, 0:8], cb[0:16, 0:8], cb[0:16, 8:16])],
                           SHM["f01"], 2)
    else:
        ex = emit_exchange(k, B, R, 0, [(SHM["c16c"][0:16, 0:8], cb[0:16, 0:8], cb[0:16, 8:16])],
                           SHM["f01"], 4)
    k.wait("V", k.EX, ex)
    def scan2(V):
        V.tensor_tensor_scan(out=hsb[0:16, :], data0=Asb[0:16, :],
                             data1=Bxsb[0:16, :], initial=cb[0:16, 8:9],
                             op0=OP.mult, op1=OP.add)
        V.drain()
        V.tensor_copy(out=cb[0:16, 1:2], in_=hsb[0:16, 2047:2048]).then_inc(k.V.h, 1)
        V.drain()
    k.op("V", scan2)
    v_s2 = k.V.bump()

    if is_pfx:
        def endm(V):
            V.tensor_tensor(out=cb[0:16, 0:1], in0=cb[0:16, 1:2],
                            in1=B["hm16"][0:16, 0:1], op=OP.mult).then_inc(k.V.h, 1)
            V.drain()
        k.op("V", endm)
        v_em = k.V.bump()
        k.wait("G", k.V, v_em)
        ex3 = emit_exchange(k, B, R, 1,
                            [(SHM["c16b"][0:16, 0:8], cb[0:16, 0:8], cb[0:16, 8:16]),
                             (SHM["sx3"][:, :], B["myxn3"][:, :], B["haloxn"][:, :])],
                            SHM["f10"], 1)
        return ex3
    else:
        k.op("G", lambda G: G.dma_start(out=B["wout"][:, :],
                                        in_=B["woutd_ap"][:, :]).then_inc(k.G.h, 16))
        g_wo = k.G.bump(16)
        k.wait("T", k.G, g_wo)
        k.wait("V", k.T, k.T.n)
        def y16(V):
            V.tensor_tensor(out=B["m_fm"][0:16, :], in0=hsb[0:16, :],
                            in1=z16sb[0:16, :], op=OP.mult).then_inc(k.V.h, 1)
            V.drain()
        k.op("V", y16)
        v_y = k.V.bump()
        k.wait("T", k.V, v_y)
        v_ev = [None, None]
        for q in range(NT):
            for dd in range(NDC):
                par = (q * NDC + dd) % 2
                pw = B["pw1a"] if par == 0 else B["pw1b"]
                if v_ev[par] is not None:
                    k.wait("T", k.V, v_ev[par])
                def womm(T, q=q, dd=dd, pw=pw):
                    T.matmul(out=pw[:, :], lhsT=B["wout"][:, 128 * dd:128 * (dd + 1)],
                             rhs=B["m_fm"][0:16, TT * q:TT * (q + 1)],
                             start=True, stop=True).then_inc(k.T.h, 1)
                k.op("T", womm)
                t_wo = k.T.bump()
                k.wait("V", k.T, t_wo)
                def woev(V, q=q, dd=dd, pw=pw):
                    rsl = B["r"][:, 2048 * dd + TT * q:2048 * dd + TT * (q + 1)]
                    V.tensor_tensor(out=rsl, in0=rsl, in1=pw[:, :],
                                    op=OP.add).then_inc(k.V.h, 1)
                k.op("V", woev)
                v_ev[par] = k.V.bump()
        return k.V.n


def build_program():
    nc = bass.Bass("TRN2", target_bir_lowering=False, detect_race_conditions=False)
    xo = nc.dram_tensor("xo", [D, TOK], F32, kind="ExternalInput")
    xp = nc.dram_tensor("xp", [D, TOK], F32, kind="ExternalInput")
    w1d = nc.dram_tensor("w1d", [NEXP, D, HID], F32, kind="ExternalInput")
    w2d = nc.dram_tensor("w2d", [NEXP, HID, D], F32, kind="ExternalInput")
    rmsd = nc.dram_tensor("rmsd", [128, 128], F32, kind="ExternalInput")
    routerd = nc.dram_tensor("routerd", [D, 14], F32, kind="ExternalInput")
    identd = nc.dram_tensor("identd", [128, 128], F32, kind="ExternalInput")
    wind = nc.dram_tensor("wind", [D, DIN], F32, kind="ExternalInput")
    wzd = nc.dram_tensor("wzd", [D, 16], F32, kind="ExternalInput")
    xpdd = nc.dram_tensor("xpdd", [HID, 80], F32, kind="ExternalInput")
    wconvd = nc.dram_tensor("wconvd", [128, 64], F32, kind="ExternalInput")
    bconvd = nc.dram_tensor("bconvd", [128, 16], F32, kind="ExternalInput")
    binbd = nc.dram_tensor("binbd", [128, 17], F32, kind="ExternalInput")
    bdtd = nc.dram_tensor("bdtd", [128, 1], F32, kind="ExternalInput")
    woutd = nc.dram_tensor("woutd", [16, 1024], F32, kind="ExternalInput")
    laned = nc.dram_tensor("laned", [1, 8], I32, kind="ExternalInput")
    ownhmd = nc.dram_tensor("ownhmd", [128, 1], F32, kind="ExternalInput")
    pfxhmd = nc.dram_tensor("pfxhmd", [128, 1], F32, kind="ExternalInput")
    hm16d = nc.dram_tensor("hm16d", [16, 1], F32, kind="ExternalInput")
    o = nc.dram_tensor("o", [TOK, D], F32, kind="ExternalOutput")
    SHM = {
        "sx1": nc.dram_tensor("sx1", [128, 24], BF16, addr_space="Shared").ap(),
        "sx3": nc.dram_tensor("sx3", [128, 24], BF16, addr_space="Shared").ap(),
        "sx4": nc.dram_tensor("sx4", [128, 24], BF16, addr_space="Shared").ap(),
        "c16a": nc.dram_tensor("c16a", [16, 8], F32, addr_space="Shared").ap(),
        "c16b": nc.dram_tensor("c16b", [16, 8], F32, addr_space="Shared").ap(),
        "c16c": nc.dram_tensor("c16c", [16, 8], F32, addr_space="Shared").ap(),
        "f01": nc.dram_tensor("f01", [1, 8], I32, addr_space="Shared").ap(),
        "f10": nc.dram_tensor("f10", [1, 8], I32, addr_space="Shared").ap(),
    }

    es = ExitStack()
    with es:
        k = KB(nc, es)
        B = {}
        B["r"] = sb(nc, es, "r", [128, 8 * 2048], F32)
        B["g"] = sb(nc, es, "g", [128, 8 * 2048], BF16)
        B["w1s"] = sb(nc, es, "w1s", [128, 16384], BF16)
        B["w2s"] = sb(nc, es, "w2s", [128, 16384], BF16)
        B["h"] = sb(nc, es, "h", [128, 16 * TT], BF16)
        B["rb"] = B["h"][:, 0:8 * TT]
        B["sqb"] = B["h"][:, 8 * TT:16 * TT]
        B["outT"] = B["h"][:, :].bitcast(F32)[:, 0:2048]
        B["m_fm"] = sb(nc, es, "m_fm", [16, 2048], BF16)
        B["CgS"] = sb(nc, es, "CgS", [128, TT], BF16)
        B["BgS"] = sb(nc, es, "BgS", [128, TT], BF16)
        B["Ce"] = sb(nc, es, "Ce", [128, 2 * TT], BF16)
        B["gt"] = B["Ce"][:, 0:TT]
        B["scr"] = sb(nc, es, "scr", [16, 5 * TT], F32)
        B["rows"] = B["scr"]
        B["s16"] = B["scr"]
        B["rowsb"] = sb(nc, es, "rowsb", [1, 3 * TT], BF16)
        B["mrow"] = sb(nc, es, "mrow", [1, 2 * TT], BF16)
        B["lgsb"] = sb(nc, es, "lgsb", [14, TT], F32)
        B["tm"] = sb(nc, es, "tm", [128, 64], F32)
        B["mtm"] = sb(nc, es, "mtm", [128, 64], F32)
        B["vt8"] = sb(nc, es, "vt8", [128, 8], F32)
        B["ident"] = sb(nc, es, "ident", [128, 128], F32)
        B["ones128b"] = sb(nc, es, "ones128b", [128, 1], BF16)
        B["onesb"] = sb(nc, es, "onesb", [1, 128], BF16)
        B["routerb"] = sb(nc, es, "routerb", [128, 14 * NDC], BF16)
        B["negcs"] = sb(nc, es, "negcs", [1, 14], BF16)
        B["rms"] = sb(nc, es, "rms", [128, 128], F32)
        B["lneps"] = sb(nc, es, "lneps", [1, 1], F32)
        B["rmseps"] = sb(nc, es, "rmseps", [1, 1], F32)
        B["wzT"] = sb(nc, es, "wzT", [128, 128], BF16)
        B["xpd"] = sb(nc, es, "xpd", [128, 16 * 80], BF16)
        B["wconv"] = sb(nc, es, "wconv", [128, 64], F32)
        B["bconv"] = sb(nc, es, "bconv", [128, 16], F32)
        B["binb"] = sb(nc, es, "binb", [128, 17], F32)
        B["bdt"] = sb(nc, es, "bdt", [128, 1], F32)
        B["wout"] = B["h"][0:16, 6656:7680]
        B["myxn3"] = sb(nc, es, "myxn3", [128, 24], BF16)
        B["haloxn"] = sb(nc, es, "haloxn", [128, 24], BF16)
        B["carrybuf"] = sb(nc, es, "carrybuf", [16, 16], F32)
        B["one16"] = sb(nc, es, "one16", [16, 1], F32)
        B["hm16"] = sb(nc, es, "hm16", [16, 1], F32)
        B["ownhm"] = sb(nc, es, "ownhm", [128, 1], F32)
        B["pfxhm"] = sb(nc, es, "pfxhm", [128, 1], F32)
        B["lansb"] = sb(nc, es, "lansb", [1, 8], I32)
        R = {"fl": sb(nc, es, "flg", [1, 8], I32),
             "fv": sb(nc, es, "fvw", [1, 8], I32)}
        B["pw1a"] = psum(nc, es, "pw1a", [128, 512])
        B["pw1b"] = psum(nc, es, "pw1b", [128, 512])
        B["pw2a"] = psum(nc, es, "pw2a", [128, 1024])
        B["pw2b"] = psum(nc, es, "pw2b", [128, 1024])
        B["pmisc"] = psum(nc, es, "pmisc", [128, 512])
        B["pmisc2"] = psum(nc, es, "pmisc2", [128, 512])

        B["woutd_ap"] = woutd.ap()
        for dst, srcd in [(B["ident"], identd), (B["rms"], rmsd),
                          (B["wconv"], wconvd), (B["bconv"], bconvd),
                          (B["binb"], binbd), (B["bdt"], bdtd),
                          (B["lansb"], laned), (B["ownhm"], ownhmd),
                          (B["pfxhm"], pfxhmd), (B["hm16"], hm16d)]:
            k.op("S", lambda S, dst=dst, srcd=srcd: S.dma_start(
                out=dst[:, :], in_=srcd[:, :]).then_inc(k.S.h, 16))
            k.S.bump(16)
        k.op("G", lambda G: G.dma_start(
            out=B["routerb"][:, :].rearrange("p (c e) -> p c e", c=8),
            in_=routerd[:, :].rearrange("(c p) e -> p c e", p=128)).then_inc(k.G.h, 16))
        k.G.bump(16)
        k.op("G", lambda G: G.dma_start(
            out=B["wzT"][:, :].rearrange("p (c e) -> p c e", c=8),
            in_=wzd[:, :].rearrange("(c p) e -> p c e", p=128)).then_inc(k.G.h, 16))
        k.G.bump(16)
        k.op("G", lambda G: G.dma_start(
            out=B["xpd"][:, :].rearrange("p (c e) -> p c e", c=16),
            in_=xpdd[:, :].rearrange("(c p) e -> p c e", p=128)).then_inc(k.G.h, 16))
        k.G.bump(16)

        k.op("S", lambda S: S.dma_start(
            out=B["r"][:, :].rearrange("p (c t) -> p c t", c=8),
            in_=xp[:, :].rearrange("(c p) t -> p c t", p=128))
            .then_inc(k.S.h, 16))
        s_r = k.S.bump(16)
        def init_v(V):
            V.memset(B["ones128b"][:, :], 1.0)
            V.memset(B["onesb"][:, :], 1.0)
            V.memset(B["vt8"][:, :], -5.0)
            V.memset(B["one16"][:, :], 1.0)
            V.memset(B["haloxn"][:, :], 0.0)
            V.memset(B["carrybuf"][:, :], 0.0)
            V.memset(B["lneps"][:, :], LN_EPS)
            V.memset(B["rmseps"][:, :], RMS_EPS).then_inc(k.V.h, 1)
        k.op("V", init_v)
        v_init = k.V.bump()
        _sn_ginit = k.S.n
        _msv = None
        def ginit(G, _sn=_sn_ginit):
            R["rlane"] = G.alloc_register("rlane")
            R["rcnt"] = G.alloc_register("rcnt")
            R["rflag"] = G.alloc_register("rflag")
            R["rdone"] = G.alloc_register("rdone")
            G.wait_ge(k.S.h, _sn)
            G.load(R["rlane"], B["lansb"][0:1, 0:1])
            G.reg_mov(R["rcnt"], 0)
            G.memset(R["fl"][:, :], 0).then_inc(k.MS.h, 1)
            G.wait_ge(k.MS.h, _msv_v[0])
            G.dma_start(out=SHM["f01"][:, :], in_=R["fl"][:, :]).then_inc(k.G.h, 16)
            G.dma_start(out=SHM["f10"][:, :], in_=R["fl"][:, :]).then_inc(k.G.h, 16)
            G.wait_ge(k.G.h, 48 + 32)
        _msv_v = [k.MS.bump()]
        k.op("G", ginit)
        k.G.bump(32)
        k.wait("T", k.G, k.G.n)
        k.wait("T", k.V, v_init)
        def cs_mm(T):
            for c in range(NDC):
                ins = T.matmul(out=B["pmisc"][0:1, 0:14], lhsT=B["ones128b"][:, :],
                               rhs=B["routerb"][:, 14 * c:14 * (c + 1)],
                               start=(c == 0), stop=(c == NDC - 1))
            ins.then_inc(k.T.h, 1)
        k.op("T", cs_mm)
        t_cs = k.T.bump()
        k.wait("V", k.T, t_cs)
        k.op("V", lambda V: V.tensor_scalar(out=B["negcs"][0:1, :], in0=B["pmisc"][0:1, 0:14],
                                            scalar1=-1.0, scalar2=None, op0=OP.mult)
             .then_inc(k.V.h, 1))
        v_cs = k.V.bump()
        k.wait("V", k.S, s_r)
        barrier(k)

        def ffn_call(first):
            emit_ffn_pass1(k, B, 1)
            wctr = {"t_w1_done": None, "t_w2_done": None,
                    "v_ce": [k.V.n, k.V.n],
                    "v_ev": [None, None], "v_g_ready": k.V.n,
                    "t_mrow": [None, None]}
            emit_ffn_pass2(k, B, [w1d[e] for e in range(NEXP)],
                           [w2d[e] for e in range(NEXP)], wctr)

        def load_winT():
            for half in range(2):
                k.op("G", lambda G, half=half: G.dma_start(
                    out=B["w1s"][:, :].rearrange("p (c h) -> p c h", c=8)[:, 4 * half:4 * (half + 1), :],
                    in_=wind[:, :].rearrange("(c p) h -> p c h", p=128)[:, 4 * half:4 * (half + 1), :])
                    .then_inc(k.G.h, 16))
                k.G.bump(16)
            k.wait("T", k.G, k.G.n)
            k.wait("A", k.G, k.G.n)

        if PH_PFX:
            ffn_call(True)
            barrier(k)
            emit_ln_pass(k, B)
            barrier(k)
            ex1 = emit_exchange(k, B, R, 0, [(SHM["sx1"][:, :], B["myxn3"][:, :],
                                              B["haloxn"][:, :])], SHM["f01"], 1)
            k.wait("V", k.EX, ex1)
            k.wait("T", k.EX, ex1)
            load_winT()
            emit_front(k, B, R, SHM, True)
            barrier(k)
        k.op("S", lambda S: S.dma_start(
            out=B["r"][:, :].rearrange("p (c t) -> p c t", c=8),
            in_=xo[:, :].rearrange("(c p) t -> p c t", p=128))
            .then_inc(k.S.h, 16))
        k.S.bump(16)
        barrier(k)
        ffn_call(False)
        if PH_MAMBA:
            barrier(k)
            emit_ln_pass(k, B)
            barrier(k)
            ex4 = emit_exchange(k, B, R, 0, [(SHM["sx4"][:, :], B["myxn3"][:, :],
                                              B["haloxn"][:, :])], SHM["f01"], 3)
            k.wait("V", k.EX, ex4)
            k.wait("T", k.EX, ex4)
            load_winT()
            emit_front(k, B, R, SHM, False)
            barrier(k)
        if PH_FFN2:
            ffn_call(False)
        barrier(k)
        emit_output(k, B, o)

        with nc.Block() as block:
            @block.sync
            def _(S):
                for fn in k.prog["S"]:
                    fn(S)

            @block.tensor
            def _(T):
                for fn in k.prog["T"]:
                    fn(T)

            @block.vector
            def _(V):
                for fn in k.prog["V"]:
                    fn(V)

            @block.scalar
            def _(A):
                for fn in k.prog["A"]:
                    fn(A)

            @block.gpsimd
            def _(G):
                for fn in k.prog["G"]:
                    fn(G)

    return nc


_NC_CACHE = None


def kernel(L0_attn_x, L0_attn_w_in, L0_attn_b_in, L0_attn_w_conv, L0_attn_b_conv,
           L0_attn_w_xproj, L0_attn_w_dt, L0_attn_b_dt, L0_attn_w_out,
           L0_ffn_router_w, L0_ffn_rms_w, L0_ffn_w1, L0_ffn_w2,
           L0_ffn_rms_w_s, L0_ffn_w1_s, L0_ffn_w2_s):
    x = np.asarray(L0_attn_x, np.float32)
    w1 = np.concatenate([np.asarray(L0_ffn_w1, np.float32),
                         np.asarray(L0_ffn_w1_s, np.float32)], 0)
    w2 = np.concatenate([np.asarray(L0_ffn_w2, np.float32),
                         np.asarray(L0_ffn_w2_s, np.float32)], 0)
    rms = np.concatenate([np.asarray(L0_ffn_rms_w, np.float32),
                          np.asarray(L0_ffn_rms_w_s, np.float32)], 0)
    rmsT = np.zeros((128, 128), np.float32)
    for e in range(NEXP):
        for c in range(NDC):
            rmsT[:, 8 * e + c] = rms[e, 128 * c:128 * (c + 1)]
    router = np.asarray(L0_ffn_router_w, np.float32)
    ident = np.eye(128, dtype=np.float32)
    w_in = np.asarray(L0_attn_w_in, np.float32)
    wind = np.ascontiguousarray(w_in[:2048].T)
    wzd = np.ascontiguousarray(w_in[2048:2064].T)
    wxp = np.asarray(L0_attn_w_xproj, np.float32)
    wdt = np.asarray(L0_attn_w_dt, np.float32)
    xpdd = np.zeros((2048, 80), np.float32)
    xpdd[:, 0:16] = wxp[0:16].T
    xpdd[:, 32:48] = wxp[16:32].T
    xpdd[:, 64:80] = wdt.T
    wcv = np.asarray(L0_attn_w_conv, np.float32)
    wconvd = np.zeros((128, 64), np.float32)
    bconvd = np.zeros((128, 16), np.float32)
    bcv = np.asarray(L0_attn_b_conv, np.float32)
    for mm in range(16):
        for dd in range(4):
            wconvd[:, 4 * mm + dd] = wcv[128 * mm:128 * (mm + 1), 0, dd]
        bconvd[:, mm] = bcv[128 * mm:128 * (mm + 1)]
    b_in = np.asarray(L0_attn_b_in, np.float32)
    binbd = np.zeros((128, 17), np.float32)
    for cc in range(16):
        binbd[:, cc] = b_in[128 * cc:128 * (cc + 1)]
    binbd[0:16, 16] = b_in[2048:2064]
    bdtd = np.zeros((128, 1), np.float32)
    bdtd[64:80, 0] = np.asarray(L0_attn_b_dt, np.float32)
    woutd = np.ascontiguousarray(np.asarray(L0_attn_w_out, np.float32)[:, :16].T)

    global _NC_CACHE
    if _NC_CACHE is None:
        _NC_CACHE = build_program()
    nc = _NC_CACHE

    in_maps = []
    for c in range(8):
        p, l = c // 2, c % 2
        b, hh = p // 2, p % 2
        own = x[b, 4096 * hh + 2048 * l: 4096 * hh + 2048 * (l + 1)]
        pfx = x[b, 2048 * l: 2048 * (l + 1)]
        in_maps.append(dict(
            xo=np.ascontiguousarray(own.T), xp=np.ascontiguousarray(pfx.T),
            w1d=w1, w2d=w2, rmsd=rmsT, routerd=router, identd=ident,
            wind=wind, wzd=wzd, xpdd=xpdd, wconvd=wconvd, bconvd=bconvd,
            binbd=binbd, bdtd=bdtd, woutd=woutd,
            laned=np.full((1, 8), l, np.int32),
            ownhmd=np.full((128, 1), float(hh) if l == 0 else 1.0, np.float32),
            pfxhmd=np.full((128, 1), 0.0 if l == 0 else 1.0, np.float32),
            hm16d=np.full((16, 1), float(hh), np.float32),
        ))
    res = run_bass_kernel_spmd(nc, in_maps, core_ids=list(range(8)))
    out = np.empty((2, 8192, 1024), np.float32)
    for c in range(8):
        p, l = c // 2, c % 2
        b, hh = p // 2, p % 2
        out[b, 4096 * hh + 2048 * l: 4096 * hh + 2048 * (l + 1)] = \
            np.asarray(res.results[c]["o"], np.float32)
    return out
